# version 05
# baseline (speedup 1.0000x reference)
import contextlib
import numpy as np
import ml_dtypes
import concourse.bass as bass
import concourse.mybir as mybir
from concourse.bass_utils import run_bass_kernel_spmd

F32 = mybir.dt.float32
BF16 = mybir.dt.bfloat16
I32 = mybir.dt.int32
AF = mybir.ActivationFunctionType
ALU = mybir.AluOpType
AX = mybir.AxisListType

TC = 1024
TH = 512
BF = ml_dtypes.bfloat16
ORDER8 = [0, 4, 1, 5, 2, 6, 3, 7]


def _interleave_perm(n_feat):
    n_chunks = n_feat // 128
    c = np.arange(n_chunks)[:, None]
    p = np.arange(128)[None, :]
    g = 16 * (c // 16) + (p % 16)
    return (g * 128 + (c % 16) * 8 + (p // 16)).reshape(-1)


def _interleave_perm_i(n=11008):
    perm = np.zeros(n, dtype=np.int64)
    c = np.arange(80)[:, None]
    p = np.arange(128)[None, :]
    g = 16 * (c // 16) + (p % 16)
    perm[: 80 * 128] = (g * 128 + (c % 16) * 8 + (p // 16)).reshape(-1)
    c2 = np.arange(80, 86)[:, None]
    perm[80 * 128:] = (c2 * 128 + (p % 16) * 8 + (p // 16)).reshape(-1)
    return perm


def _word_gather_map(perm, n=None):
    n = len(perm) if n is None else n
    wm = np.zeros(n // 8, dtype=np.int64)
    for c in range(n // 128):
        wm[c * 16:(c + 1) * 16] = perm[c * 128: c * 128 + 16] // 8
    return wm


def prep_all(inputs, n_cores=8):
    P = _interleave_perm(4096)
    PI = _interleave_perm_i(11008)
    WM_P = _word_gather_map(P)
    WM_I = _word_gather_map(PI)
    S = {}
    f32 = np.float32

    w = np.asarray(inputs["L0_attn_weight"])
    m = np.asarray(inputs["L0_attn_mask"])
    S["l0_w"] = np.ascontiguousarray(w[P][:, P].T)
    S["l0_m"] = np.ascontiguousarray(m[P][:, P].T)

    A = np.asarray(inputs["L0_ffn_A"])
    B = np.asarray(inputs["L0_ffn_B"])
    Wr = np.asarray(inputs["L0_ffn_Wr"])
    at = A[:, :, P].transpose(2, 0, 1).reshape(4096, 64)
    S["l0f_at"] = np.ascontiguousarray(
        at.reshape(32, 128, 64).transpose(1, 0, 2).reshape(128, 2048)).astype(f32)
    S["l0f_bt"] = np.ascontiguousarray(
        B[:, P, :].transpose(0, 2, 1).reshape(4, 16, 4096).transpose(1, 0, 2)
        .reshape(16, 4 * 4096)).astype(f32)
    wrt = Wr[:, P].T
    S["l0f_wrt"] = np.ascontiguousarray(
        wrt.reshape(32, 128, 4).transpose(1, 0, 2).reshape(128, 128)).astype(f32)

    S["qkv_qw"] = np.ascontiguousarray(np.asarray(inputs["L1_attn_qkv_qw"])[P][:, WM_P])
    S["qkv_qz"] = np.ascontiguousarray(np.asarray(inputs["L1_attn_qkv_qz"])[:, WM_P])
    S["qkv_sc"] = np.ascontiguousarray(np.asarray(inputs["L1_attn_qkv_sc"])[:, :4096][:, P])

    S["o_qw"] = np.ascontiguousarray(np.asarray(inputs["L1_attn_o_qw"])[P][:, WM_P])
    S["o_qz"] = np.ascontiguousarray(np.asarray(inputs["L1_attn_o_qz"])[:, WM_P])
    S["o_sc"] = np.ascontiguousarray(np.asarray(inputs["L1_attn_o_sc"])[:, P])

    wm_g = WM_I.reshape(86, 16)
    wm_gu = np.stack([wm_g, 1376 + wm_g], axis=1).reshape(-1)
    cm_g = PI.reshape(86, 128)
    cm_gu = np.stack([cm_g, 11008 + cm_g], axis=1).reshape(-1)
    S["gu_qw"] = np.ascontiguousarray(np.asarray(inputs["L1_attn_gu_qw"])[P][:, wm_gu])
    S["gu_qz"] = np.ascontiguousarray(np.asarray(inputs["L1_attn_gu_qz"])[:, wm_gu])
    S["gu_sc"] = np.ascontiguousarray(np.asarray(inputs["L1_attn_gu_sc"])[:, cm_gu])

    S["dn_qw"] = np.ascontiguousarray(np.asarray(inputs["L1_attn_dn_qw"])[PI][:, WM_P])
    S["dn_qz"] = np.ascontiguousarray(np.asarray(inputs["L1_attn_dn_qz"])[:, WM_P])
    S["dn_sc"] = np.ascontiguousarray(np.asarray(inputs["L1_attn_dn_sc"])[:, P])

    S["ln1_w"] = np.ascontiguousarray(np.asarray(inputs["L1_attn_ln1_w"])[P].reshape(32, 128).T).astype(f32)
    S["ln2_w"] = np.ascontiguousarray(np.asarray(inputs["L1_attn_ln2_w"])[P].reshape(32, 128).T).astype(f32)

    S["f_qg"] = np.ascontiguousarray(np.asarray(inputs["L1_ffn_qw_gate"])[PI][:, P].T)
    S["f_qu"] = np.ascontiguousarray(np.asarray(inputs["L1_ffn_qw_up"])[PI][:, P].T)
    S["f_qd"] = np.ascontiguousarray(np.asarray(inputs["L1_ffn_qw_down"])[P][:, PI].T)
    S["f_sg"] = np.ascontiguousarray(np.asarray(inputs["L1_ffn_s_gate"])[PI].T).astype(f32)
    S["f_zg"] = np.ascontiguousarray(np.asarray(inputs["L1_ffn_z_gate"])[PI].T).astype(f32)
    S["f_su"] = np.ascontiguousarray(np.asarray(inputs["L1_ffn_s_up"])[PI].T).astype(f32)
    S["f_zu"] = np.ascontiguousarray(np.asarray(inputs["L1_ffn_z_up"])[PI].T).astype(f32)
    S["f_sd"] = np.ascontiguousarray(np.asarray(inputs["L1_ffn_s_down"])[P].T).astype(f32)
    S["f_zd"] = np.ascontiguousarray(np.asarray(inputs["L1_ffn_z_down"])[P].T).astype(f32)

    R16 = np.zeros((16, 128), dtype=BF)
    R16[np.arange(128) % 16, np.arange(128)] = 1.0
    S["c_r16"] = R16
    S["c_ones1"] = np.ones((1, 128), dtype=BF)
    S["c_ones128"] = np.ones((128, 1), dtype=BF)
    S["c_ident"] = np.eye(128, dtype=BF)

    x0 = np.asarray(inputs["L0_attn_data"]).reshape(8192, 4096)
    x0_fm = np.ascontiguousarray(x0[:, P].T)
    cores = [{"x0": np.ascontiguousarray(x0_fm[:, c * TC:(c + 1) * TC])} for c in range(n_cores)]
    return S, cores, P, PI



ENGS = ("vector", "scalar", "tensor", "gpsimd", "sync")
GUARD_SEMS = ("v", "t", "s", "g", "o0", "o1")


class Emit:

    def __init__(self, nc, sems):
        self.nc = nc
        self.sems = sems
        self.base = {k: 0 for k in sems}
        self.cnt = dict(self.base)
        self.cur = None
        self.eng = None

    def run_phase(self, phase_fn, *args):
        snap = dict(self.base)
        with self.nc.Block() as blk:
            for name in ENGS:
                self.cur = name
                self.cnt = dict(snap)

                def body(eng, fn=phase_fn, a=args):
                    self.eng = eng
                    fn(self, *a)
                getattr(blk, name)(body)
        self.base = dict(self.cnt)

    def snap(self):
        return dict(self.cnt)

    def guard(self):
        return tuple((k, self.cnt[k]) for k in GUARD_SEMS)

    def op(self, engine, method, *args, waits=(), incs=(), **kw):
        if self.cur == engine:
            for (s, v) in waits:
                if v > 0:
                    self.eng.wait_ge(self.sems[s], v)
            ins = getattr(self.eng, method)(*args, **kw)
            for (s, n) in incs:
                ins = ins.then_inc(self.sems[s], n)
        for (s, n) in incs:
            self.cnt[s] += n

    def v(self, method, *a, **k):
        self.op("vector", method, *a, **k)

    def s(self, method, *a, **k):
        self.op("scalar", method, *a, **k)

    def t(self, method, *a, **k):
        self.op("tensor", method, *a, **k)

    def g(self, method, *a, **k):
        self.op("gpsimd", method, *a, **k)

    def dma(self, queue, out, in_, waits=(), sem="w", **kw):
        self.op(queue, "dma_start", out=out, in_=in_, waits=waits,
                incs=((sem, 16),), **kw)


def bcast_free(ap, reps):
    r = ap.rearrange("p (c f) -> p c f", c=1)
    return bass.AP(r.tensor, r.offset, [r.ap[0], [0, reps], r.ap[2]])



SUBBLOCKS = ("l0a", "l0f", "l1a", "l1f")


def build_launch(sub_blocks, is_first, is_last):
    nc = bass.Bass("TRN2", target_bir_lowering=False)
    st = contextlib.ExitStack()
    D = {}

    def din(name, shape, dtype):
        D[name] = nc.dram_tensor(name, list(shape), dtype, kind="ExternalInput").ap()

    if is_first:
        din("x0", (4096, TC), F32)
    else:
        din("x_in", (4096, TC), BF16)
    D["x_out"] = nc.dram_tensor("x_out", [4096, TC], F32 if is_last else BF16,
                                kind="ExternalOutput").ap()
    for i in range(len(sub_blocks) - 1):
        D[f"xb{i}"] = nc.dram_tensor(f"xb{i}", [4096, TC], BF16, kind="Internal").ap()
    D["xr_buf"] = nc.dram_tensor("xr_buf", [4096, TH], BF16, kind="Internal").ap()
    D["dbg"] = nc.dram_tensor("dbg", [128, 128], F32, kind="ExternalOutput").ap()
    D["dbg2"] = nc.dram_tensor("dbg2", [4096, TH], BF16, kind="ExternalOutput").ap()

    din("c_r16", (16, 128), BF16)
    din("c_ones1", (1, 128), BF16)
    din("c_ones128", (128, 1), BF16)
    din("c_ident", (128, 128), BF16)

    if "l0a" in sub_blocks:
        din("l0_w", (4096, 4096), F32)
        din("l0_m", (4096, 4096), I32)
    if "l0f" in sub_blocks:
        din("l0f_at", (128, 2048), F32)
        din("l0f_bt", (16, 4 * 4096), F32)
        din("l0f_wrt", (128, 128), F32)
    if "l1a" in sub_blocks:
        din("qkv_qw", (4096, 512), I32)
        din("qkv_qz", (32, 512), I32)
        din("qkv_sc", (32, 4096), F32)
        din("o_qw", (4096, 512), I32)
        din("o_qz", (32, 512), I32)
        din("o_sc", (32, 4096), F32)
        din("gu_qw", (4096, 2752), I32)
        din("gu_qz", (32, 2752), I32)
        din("gu_sc", (32, 22016), F32)
        din("dn_qw", (11008, 512), I32)
        din("dn_qz", (86, 512), I32)
        din("dn_sc", (86, 4096), F32)
        din("ln1_w", (128, 32), F32)
        din("ln2_w", (128, 32), F32)
    if "l1f" in sub_blocks:
        din("f_qg", (4096, 11008), I32)
        din("f_qu", (4096, 11008), I32)
        din("f_qd", (11008, 4096), I32)
        for k in ("f_sg", "f_zg", "f_su", "f_zu"):
            din(k, (32, 11008), F32)
        for k in ("f_sd", "f_zd"):
            din(k, (86, 4096), F32)

    def sb_(name, shape, dt, stack=st):
        return stack.enter_context(nc.sbuf_tensor(name, list(shape), dt))
    D["sb"] = sb_
    D["A0"] = sb_("A0", (128, 86 * TH), BF16)
    D["r16"] = sb_("r16", (16, 128), BF16)
    D["ones1"] = sb_("ones1", (1, 128), BF16)
    D["ones128"] = sb_("ones128", (128, 1), BF16)
    D["ident"] = sb_("ident", (128, 128), BF16)
    D["rows"] = sb_("rows", (1, 4 * TH), F32)
    D["ceps"] = sb_("ceps", (1, 2), F32)
    D["rows_bf"] = sb_("rows_bf", (1, 2 * TH), BF16)
    D["vdummy"] = sb_("vdummy", (1, 8), F32)

    sems = {}
    for s_ in ("c", "xh", "t", "v", "s", "g",
               "w0", "w1", "x0", "x1", "o0", "o1", "r0", "r1", "ga", "gb"):
        sems[s_] = st.enter_context(nc.semaphore("sem_" + s_))
    E = Emit(nc, sems)

    def ph_prologue(E):
        E.v("memset", D["ceps"][0:1, 0:1], 1e-5)
        E.v("memset", D["ceps"][0:1, 1:2], 1e-6, incs=(("v", 1),))
        E.dma("sync", D["r16"][:], D["c_r16"], sem="c")
        E.dma("sync", D["ones1"][:], D["c_ones1"], sem="c")
        E.dma("sync", D["ones128"][:], D["c_ones128"], sem="c")
        E.dma("sync", D["ident"][:], D["c_ident"], sem="c")
    E.run_phase(ph_prologue)
    E.ctok = ("c", 64)

    for bi, sbname in enumerate(sub_blocks):
        x_src = ("x0" if is_first else "x_in") if bi == 0 else f"xb{bi - 1}"
        x_dst = "x_out" if bi == len(sub_blocks) - 1 else f"xb{bi}"
        final = (bi == len(sub_blocks) - 1) and is_last
        EMITTERS[sbname](E, nc, D, x_src, x_dst, bi == 0 and is_first, final)

    fin = E.snap()
    dma_sems = ("c", "xh", "w0", "w1", "x0", "x1", "o0", "o1", "r0", "r1", "ga", "gb")

    def ph_epilogue(E):
        if E.cur == "sync":
            for s_ in dma_sems:
                if fin[s_] > 0:
                    E.eng.wait_ge(E.sems[s_], fin[s_])
    E.run_phase(ph_epilogue)
    st.close()
    return nc



def load_xh(E, D, xh, x_src, h, f32_src, x0_t):
    guard = E.guard()
    src = D[x_src].rearrange("(c p) t -> p c t", p=128)
    if f32_src:
        vtok = {}
        for kc in range(32):
            par = kc % 2
            w_ = guard if kc < 2 else (("v", vtok[kc - 2]),)
            sk = f"x{par}"
            E.dma("scalar", x0_t[par][:], src[:, kc, h * TH:(h + 1) * TH], waits=w_, sem=sk)
            E.v("tensor_scalar", out=xh[:, kc * TH:(kc + 1) * TH], in0=x0_t[par][:],
                scalar1=1.0, scalar2=None, op0=ALU.mult,
                waits=((sk, E.cnt[sk]),), incs=(("v", 1),))
            vtok[kc] = E.cnt["v"]
        return ("v", E.cnt["v"])
    else:
        E.dma("scalar", xh[:].rearrange("p (c t) -> p c t", c=32),
              src[:, :, h * TH:(h + 1) * TH], waits=guard, sem="xh")
        return ("xh", E.cnt["xh"])


def stats_ln(E, D, s_tile, n_chunks, eps, stok, sq, ps1, ps2, pspan,
             mu_pan, r_pan, with_mean=True, denom=4096.0):
    rows = D["rows"]
    rows_bf = D["rows_bf"]
    ttoks = {}
    for kc in range(n_chunks):
        w_ = [stok] if kc == 0 else []
        if kc >= 2:
            w_.append(("t", ttoks[kc - 2]))
        E.s("activation", out=sq[kc % 2][:], in_=s_tile[:, kc * TH:(kc + 1) * TH],
            func=AF.Square, waits=tuple(w_), incs=(("s", 1),))
        if with_mean:
            E.t("matmul", out=ps1[0:1, :], lhsT=D["ones128"][:],
                rhs=s_tile[:, kc * TH:(kc + 1) * TH],
                start=(kc == 0), stop=(kc == n_chunks - 1),
                waits=(stok, ("c", 64)) if kc == 0 else (), skip_group_check=True)
        E.t("matmul", out=ps2[0:1, :], lhsT=D["ones128"][:], rhs=sq[kc % 2][:],
            start=(kc == 0), stop=(kc == n_chunks - 1),
            waits=(("s", E.cnt["s"]),) + ((("c", 64),) if kc == 0 else ()),
            skip_group_check=True, incs=(("t", 1),))
        ttoks[kc] = E.cnt["t"]
    ttok = E.cnt["t"]
    MU, MS2, MU2, VAR = 0, 1, 2, 3
    R = lambda i: rows[0:1, i * TH:(i + 1) * TH]
    if with_mean:
        E.v("tensor_scalar", out=R(MU), in0=ps1[0:1, :], scalar1=1.0 / denom, scalar2=None,
            op0=ALU.mult, waits=(("t", ttok),))
    E.v("tensor_scalar", out=R(MS2), in0=ps2[0:1, :], scalar1=1.0 / denom, scalar2=None,
        op0=ALU.mult, waits=(("t", ttok),), incs=(("v", 1),))
    if E.cur == "vector":
        E.eng.wait_ge(E.sems["v"], E.cnt["v"])
    if with_mean:
        E.v("tensor_mul", out=R(MU2), in0=R(MU), in1=R(MU), incs=(("v", 1),))
        if E.cur == "vector":
            E.eng.wait_ge(E.sems["v"], E.cnt["v"])
        E.v("tensor_sub", out=R(VAR), in0=R(MS2), in1=R(MU2), incs=(("v", 1),))
    else:
        E.v("tensor_scalar", out=R(VAR), in0=R(MS2), scalar1=1.0, scalar2=None,
            op0=ALU.mult, incs=(("v", 1),))
    if E.cur == "vector":
        E.eng.wait_ge(E.sems["v"], E.cnt["v"])
    vtok = E.cnt["v"]
    E.s("activation", out=R(VAR), in_=R(VAR), func=AF.Ln,
        bias=D["ceps"][0:1, (0 if eps == 1e-5 else 1):(1 if eps == 1e-5 else 2)],
        waits=(("v", vtok),))
    E.s("activation", out=R(VAR), in_=R(VAR), func=AF.Exp, scale=-0.5, incs=(("s", 1),))
    stok2 = E.cnt["s"]
    E.v("tensor_scalar", out=rows_bf[0:1, 0:TH], in0=R(VAR), scalar1=1.0, scalar2=None,
        op0=ALU.mult, waits=(("s", stok2),), incs=(("v", 1),))
    if with_mean:
        E.v("tensor_scalar", out=rows_bf[0:1, TH:2 * TH], in0=R(MU), scalar1=1.0, scalar2=None,
            op0=ALU.mult, incs=(("v", 1),))
    vtok = E.cnt["v"]
    E.t("matmul", out=pspan[:], lhsT=D["ones1"][:], rhs=rows_bf[0:1, 0:TH],
        start=True, stop=True, waits=(("v", vtok),), incs=(("t", 1),))
    ttok = E.cnt["t"]
    E.v("tensor_scalar", out=r_pan[:], in0=pspan[:], scalar1=1.0, scalar2=None,
        op0=ALU.mult, waits=(("t", ttok),), incs=(("v", 1),))
    if with_mean:
        E.t("matmul", out=pspan[:], lhsT=D["ones1"][:], rhs=rows_bf[0:1, TH:2 * TH],
            start=True, stop=True, waits=(("v", E.cnt["v"]),), incs=(("t", 1),))
        ttok = E.cnt["t"]
        E.v("tensor_scalar", out=mu_pan[:], in0=pspan[:], scalar1=1.0, scalar2=None,
            op0=ALU.mult, waits=(("t", ttok),), incs=(("v", 1),))
    return ("v", E.cnt["v"])


DEBUG_SKIP_LN = False


def emit_ln_out(E, D, s_tile, x_dst, h, final, mu_pan, r_pan, ptok, fout, scratch):
    dst = D[x_dst].rearrange("(c p) t -> p c t", p=128)
    otok = {}
    if DEBUG_SKIP_LN:
        for kc in range(32):
            w_ = (ptok,) if kc == 0 else ()
            if kc >= 2:
                w_ = w_ + ((f"o{kc % 2}", otok[kc - 2]),)
            tmp = fout[kc % 2]
            E.v("tensor_scalar", out=tmp[:], in0=s_tile[:, kc * TH:(kc + 1) * TH],
                scalar1=1.0, scalar2=None, op0=ALU.mult, waits=w_, incs=(("v", 1),))
            E.dma("gpsimd", dst[:, kc, h * TH:(h + 1) * TH], tmp[:],
                  waits=(("v", E.cnt["v"]),), sem=f"o{kc % 2}")
            otok[kc] = E.cnt[f"o{kc % 2}"]
        return
    if not final:
        for kc in range(32):
            src = s_tile[:, kc * TH:(kc + 1) * TH]
            w_ = (ptok,) if kc == 0 else ()
            if kc >= 2:
                w_ = w_ + ((f"o{kc % 2}", otok[kc - 2]),)
            tmp = scratch[kc % 2]
            E.v("tensor_sub", out=tmp[:], in0=src, in1=mu_pan[:], waits=w_)
            E.v("tensor_mul", out=tmp[:], in0=tmp[:], in1=r_pan[:], incs=(("v", 1),))
            E.dma("gpsimd", dst[:, kc, h * TH:(h + 1) * TH], tmp[:],
                  waits=(("v", E.cnt["v"]),), sem=f"o{kc % 2}")
            otok[kc] = E.cnt[f"o{kc % 2}"]
    else:
        for kc in range(32):
            src = s_tile[:, kc * TH:(kc + 1) * TH]
            w_ = (ptok,) if kc == 0 else ()
            if kc >= 2:
                w_ = w_ + ((f"o{kc % 2}", otok[kc - 2]),)
            tmp = fout[kc % 2]
            E.v("tensor_sub", out=tmp[:], in0=src, in1=mu_pan[:], waits=w_)
            E.v("tensor_mul", out=tmp[:], in0=tmp[:], in1=r_pan[:], incs=(("v", 1),))
            E.dma("gpsimd", dst[:, kc, h * TH:(h + 1) * TH], tmp[:],
                  waits=(("v", E.cnt["v"]),), sem=f"o{kc % 2}")
            otok[kc] = E.cnt[f"o{kc % 2}"]



def emit_l0a(E, nc, D, x_src, x_dst, f32_src, final):
    u0 = uid()
    with contextlib.ExitStack() as ph:
        sb = lambda n_, sh, dt: ph.enter_context(nc.sbuf_tensor(f"{n_}_{u0}", list(sh), dt))
        xh = sb("xh", (128, 32 * TH), BF16)
        x0_t = [sb(f"x0t{i}", (128, TH), F32) for i in range(2)] if f32_src else None
        wbuf = [sb(f"l0w{i}", (128, 16 * 128), F32) for i in range(2)]
        mbuf = [sb(f"l0m{i}", (128, 16 * 128), I32) for i in range(2)]
        wd = [sb(f"l0d{i}", (128, 16 * 128), BF16) for i in range(2)]
        sq = [sb(f"sq{i}", (128, TH), BF16) for i in range(2)]
        mu_pan = sb("mu_pan", (128, TH), BF16)
        r_pan = sb("r_pan", (128, TH), BF16)
        fout = [sb(f"fo{i}", (128, TH), F32 if final else BF16) for i in range(2)]
        ps_m = [ph.enter_context(nc.psum_tensor(f"psm{i}_{uid()}", [128, TH], F32)) for i in range(2)]
        ps1 = ph.enter_context(nc.psum_tensor(f"ps1_{uid()}", [1, TH], F32))
        ps2 = ph.enter_context(nc.psum_tensor(f"ps2_{uid()}", [1, TH], F32))
        pspan = ph.enter_context(nc.psum_tensor(f"pspan_{uid()}", [128, TH], F32))

        w_h = D["l0_w"].rearrange("(kc p) n -> p kc n", p=128)
        m_h = D["l0_m"].rearrange("(kc p) n -> p kc n", p=128)
        s_tile = D["A0"][:, 0:32 * TH]

        def phase(E, h):
            guard = E.guard()
            xtok = load_xh(E, D, xh, x_src, h, f32_src, x0_t)
            deq_tok = {}
            evac_tok = {}
            grp_tok = {}
            bi = 0
            for nc_ in range(32):
                for hh in range(2):
                    par = bi % 2
                    wt = guard if bi < 2 else (("v", deq_tok[bi - 2]),)
                    sk = f"w{par}"
                    E.dma("sync", wbuf[par][:].rearrange("p (c n) -> p c n", c=16),
                          w_h[:, hh * 16:(hh + 1) * 16, nc_ * 128:(nc_ + 1) * 128],
                          waits=wt, sem=sk)
                    E.dma("sync", mbuf[par][:].rearrange("p (c n) -> p c n", c=16),
                          m_h[:, hh * 16:(hh + 1) * 16, nc_ * 128:(nc_ + 1) * 128], sem=sk)
                    dw = [(sk, E.cnt[sk])]
                    if bi >= 2:
                        dw.append(("t", grp_tok[nc_ - 1]))
                    E.v("tensor_tensor", out=wd[par][:], in0=wbuf[par][:], in1=mbuf[par][:],
                        op=ALU.mult, waits=tuple(dw), incs=(("v", 1),))
                    deq_tok[bi] = E.cnt["v"]
                    for kc in range(16):
                        w_ = []
                        if kc == 0:
                            w_.append(("v", deq_tok[bi]))
                            if hh == 0:
                                w_.append(xtok)
                                if nc_ >= 2:
                                    w_.append(("v", evac_tok[nc_ - 2]))
                        E.t("matmul", out=ps_m[nc_ % 2][:],
                            lhsT=wd[par][:, kc * 128:(kc + 1) * 128],
                            rhs=xh[:, (hh * 16 + kc) * TH:(hh * 16 + kc + 1) * TH],
                            start=(hh == 0 and kc == 0), stop=(hh == 1 and kc == 15),
                            waits=tuple(w_), skip_group_check=True,
                            incs=(("t", 1),) if (hh == 1 and kc == 15) else ())
                    bi += 1
                mm_tok = E.cnt["t"]
                grp_tok[nc_] = mm_tok
                E.v("tensor_tensor", out=s_tile[:, nc_ * TH:(nc_ + 1) * TH],
                    in0=ps_m[nc_ % 2][:], in1=xh[:, nc_ * TH:(nc_ + 1) * TH],
                    op=ALU.add, waits=(("t", mm_tok),), incs=(("v", 1),))
                evac_tok[nc_] = E.cnt["v"]
            stok = ("v", E.cnt["v"])
            ptok = stats_ln(E, D, s_tile, 32, 1e-5, stok, sq, ps1, ps2,
                            pspan, mu_pan, r_pan)
            emit_ln_out(E, D, s_tile, x_dst, h, final, mu_pan, r_pan, ptok, fout, fout)

        for h in range(2):
            E.run_phase(phase, h)



DEBUG_L0F = 5


def emit_l0f(E, nc, D, x_src, x_dst, f32_src, final):
    assert not f32_src
    u0 = uid()
    with contextlib.ExitStack() as ph:
        sb = lambda n_, sh, dt: ph.enter_context(nc.sbuf_tensor(f"{n_}_{u0}", list(sh), dt))
        xh = sb("xh", (128, 32 * TH), BF16)
        at_bf = sb("at_bf", (128, 2048), BF16)
        bt_bf = sb("bt_bf", (16, 4 * 4096), BF16)
        wrt_bf = sb("wrt_bf", (128, 128), BF16)
        stage = sb("stage", (128, 4096), F32)
        lg = sb("lg", (128, 16), F32)
        gt = sb("gt", (128, 16), F32)
        gt2 = sb("gt2", (128, 8), F32)
        m1 = sb("m1", (128, 4), F32)
        m2 = sb("m2", (128, 4), F32)
        nm1 = sb("nm1", (128, 4), F32)
        kp = sb("kp", (128, 16), F32)
        ex = sb("ex", (128, 16), F32)
        en = sb("en", (128, 16), F32)
        den = sb("den", (128, 4), F32)
        rden = sb("rden", (128, 4), F32)
        gates = sb("gates", (128, 16), F32)
        midp = sb("midp", (128, 64), BF16)
        mid_im = sb("mid_im", (16, 4 * TH), BF16)
        sq = [sb(f"sq{i}", (128, TH), BF16) for i in range(2)]
        mu_pan = sb("mu_pan", (128, TH), BF16)
        r_pan = sb("r_pan", (128, TH), BF16)
        fout = [sb(f"fo{i}", (128, TH), F32 if final else BF16) for i in range(2)]
        ps_m = [ph.enter_context(nc.psum_tensor(f"psm{i}_{uid()}", [128, TH], F32)) for i in range(2)]
        ps_small = ph.enter_context(nc.psum_tensor(f"ps_small_{uid()}", [128, 16], F32))
        ps_mid_t = ph.enter_context(nc.psum_tensor(f"ps_mid_t_{uid()}", [128, 64], F32))
        ps_tr = ph.enter_context(nc.psum_tensor(f"ps_tr_{uid()}", [16, 128], BF16))
        ps1 = ph.enter_context(nc.psum_tensor(f"ps1_{uid()}", [1, TH], F32))
        ps2 = ph.enter_context(nc.psum_tensor(f"ps2_{uid()}", [1, TH], F32))
        pspan = ph.enter_context(nc.psum_tensor(f"pspan_{uid()}", [128, TH], F32))
        s_tile = D["A0"][:, 0:32 * TH]

        consts_tok = [None]

        def ph_consts(E):
            guard = E.guard()
            E.dma("sync", stage[:, 0:2048], D["l0f_at"], waits=guard, sem="r0")
            E.dma("sync", stage[:, 2048:2176], D["l0f_wrt"], sem="r0")
            rtok = E.cnt["r0"]
            E.v("tensor_scalar", out=at_bf[:], in0=stage[:, 0:2048], scalar1=1.0,
                scalar2=None, op0=ALU.mult, waits=(("r0", rtok),))
            E.v("tensor_scalar", out=wrt_bf[:], in0=stage[:, 2048:2176], scalar1=1.0,
                scalar2=None, op0=ALU.mult, incs=(("v", 1),))
            for e in range(4):
                E.dma("sync", stage[0:16, 0:4096], D["l0f_bt"][:, e * 4096:(e + 1) * 4096],
                      waits=(("v", E.cnt["v"]),), sem="r1")
                E.v("tensor_scalar", out=bt_bf[:, e * 4096:(e + 1) * 4096],
                    in0=stage[0:16, 0:4096], scalar1=1.0, scalar2=None, op0=ALU.mult,
                    waits=(("r1", E.cnt["r1"]),), incs=(("v", 1),))
            consts_tok[0] = ("v", E.cnt["v"])
        E.run_phase(ph_consts)

        def phase(E, h):
            guard = E.guard()
            xtok = load_xh(E, D, xh, x_src, h, False, None)
            ctok = E.ctok

            if DEBUG_L0F == 1:
                pass
            def ident_out():
                for kc in range(32):
                    E.v("tensor_scalar", out=s_tile[:, kc * TH:(kc + 1) * TH],
                        in0=xh[:, kc * TH:(kc + 1) * TH], scalar1=1.0, scalar2=None,
                        op0=ALU.mult, waits=(xtok,) if kc == 0 else (),
                        incs=(("v", 1),) if kc == 31 else ())
                stok_ = ("v", E.cnt["v"])
                ptok_ = stats_ln(E, D, s_tile, 32, 1e-5, stok_, sq, ps1, ps2,
                                 pspan, mu_pan, r_pan)
                emit_ln_out(E, D, s_tile, x_dst, h, final, mu_pan, r_pan, ptok_, fout, fout)
            if DEBUG_L0F == 1:
                return ident_out()
            for tc in range(4):
                for kc in range(32):
                    w_ = []
                    if kc == 0:
                        w_ = [xtok, consts_tok[0]] if tc == 0 else []
                    E.t("matmul", out=ps_small[:, tc * 4:(tc + 1) * 4],
                        lhsT=xh[:, (kc * TH) + tc * 128: (kc * TH) + (tc + 1) * 128],
                        rhs=wrt_bf[:, kc * 4:(kc + 1) * 4],
                        start=(kc == 0), stop=(kc == 31), waits=tuple(w_),
                        skip_group_check=True,
                        incs=(("t", 1),) if kc == 31 else ())
            lgt = E.cnt["t"]
            E.v("tensor_scalar", out=lg[:], in0=ps_small[:, 0:16], scalar1=1.0, scalar2=None,
                op0=ALU.mult, waits=(("t", lgt),), incs=(("v", 1),))
            if E.cur == "vector":
                E.eng.wait_ge(E.sems["v"], E.cnt["v"])
            if DEBUG_L0F == 2:
                return ident_out()
            def vsync():
                E.v("memset", D["vdummy"][0:1, 0:8], 0.0, incs=(("v", 1),))
                if E.cur == "vector":
                    E.eng.wait_ge(E.sems["v"], E.cnt["v"])
            C = lambda t_, i: lg[:, t_ * 4 + i: t_ * 4 + i + 1]
            for tc in range(4):
                E.v("tensor_max", out=gt[:, tc * 4 + 0:tc * 4 + 1], in0=C(tc, 0), in1=C(tc, 1))
                E.v("tensor_max", out=gt[:, tc * 4 + 1:tc * 4 + 2], in0=C(tc, 2), in1=C(tc, 3))
                E.v("tensor_tensor", out=gt[:, tc * 4 + 2:tc * 4 + 3], in0=C(tc, 0), in1=C(tc, 1), op=ALU.min)
                E.v("tensor_tensor", out=gt[:, tc * 4 + 3:tc * 4 + 4], in0=C(tc, 2), in1=C(tc, 3), op=ALU.min)
            vsync()
            G = lambda t_, i: gt[:, t_ * 4 + i: t_ * 4 + i + 1]
            for tc in range(4):
                E.v("tensor_max", out=m1[:, tc:tc + 1], in0=G(tc, 0), in1=G(tc, 1))
                E.v("tensor_tensor", out=gt2[:, tc * 2:tc * 2 + 1], in0=G(tc, 0), in1=G(tc, 1), op=ALU.min)
                E.v("tensor_max", out=gt2[:, tc * 2 + 1:tc * 2 + 2], in0=G(tc, 2), in1=G(tc, 3))
            vsync()
            for tc in range(4):
                E.v("tensor_max", out=m2[:, tc:tc + 1], in0=gt2[:, tc * 2:tc * 2 + 1],
                    in1=gt2[:, tc * 2 + 1:tc * 2 + 2])
            vsync()
            for tc in range(4):
                E.v("tensor_scalar", out=kp[:, tc * 4:(tc + 1) * 4], in0=lg[:, tc * 4:(tc + 1) * 4],
                    scalar1=m2[:, tc:tc + 1], scalar2=None, op0=ALU.is_ge)
                E.v("tensor_scalar", out=nm1[:, tc:tc + 1], in0=m1[:, tc:tc + 1],
                    scalar1=-1.0, scalar2=None, op0=ALU.mult,
                    incs=(("v", 1),) if tc == 3 else ())
            vtok = E.cnt["v"]
            for tc in range(4):
                E.s("activation", out=ex[:, tc * 4:(tc + 1) * 4], in_=lg[:, tc * 4:(tc + 1) * 4],
                    func=AF.Exp, bias=nm1[:, tc:tc + 1],
                    waits=(("v", vtok),) if tc == 0 else (),
                    incs=(("s", 1),) if tc == 3 else ())
            stok = E.cnt["s"]
            for tc in range(4):
                E.v("tensor_mul", out=en[:, tc * 4:(tc + 1) * 4], in0=ex[:, tc * 4:(tc + 1) * 4],
                    in1=kp[:, tc * 4:(tc + 1) * 4], waits=(("s", stok),) if tc == 0 else ())
            vsync()
            for tc in range(4):
                E.v("reduce_sum", out=den[:, tc:tc + 1], in_=en[:, tc * 4:(tc + 1) * 4],
                    axis=AX.X, incs=(("v", 1),) if tc == 3 else ())
            vtok = E.cnt["v"]
            E.s("activation", out=rden[:], in_=den[:], func=AF.Ln, waits=(("v", vtok),))
            E.s("activation", out=rden[:], in_=rden[:], func=AF.Exp, scale=-1.0, incs=(("s", 1),))
            stok = E.cnt["s"]
            for tc in range(4):
                E.v("tensor_scalar", out=gates[:, tc * 4:(tc + 1) * 4], in0=en[:, tc * 4:(tc + 1) * 4],
                    scalar1=rden[:, tc:tc + 1], scalar2=4.0, op0=ALU.mult, op1=ALU.mult,
                    waits=(("s", stok),) if tc == 0 else (),
                    incs=(("v", 1),) if tc == 3 else ())
            gtok = E.cnt["v"]
            if h == 0:
                E.dma("gpsimd", D["dbg"][:, 0:16], lg[:], waits=(("v", gtok),), sem="r0")
                E.dma("gpsimd", D["dbg"][:, 16:32], gt[:], sem="r0")
                E.dma("gpsimd", D["dbg"][:, 32:36], m1[:], sem="r0")
                E.dma("gpsimd", D["dbg"][:, 36:40], m2[:], sem="r0")
                E.dma("gpsimd", D["dbg"][:, 40:56], kp[:], sem="r0")
                E.dma("gpsimd", D["dbg"][:, 56:60], den[:], sem="r0")
                E.dma("gpsimd", D["dbg"][:, 60:64], rden[:], sem="r0")
                E.dma("gpsimd", D["dbg"][:, 64:80], en[:], sem="r0")
                E.dma("gpsimd", D["dbg"][:, 80:96], gates[:], sem="r0")
            if DEBUG_L0F == 3:
                return ident_out()
            for tc in range(4):
                for kc in range(32):
                    w_ = []
                    if kc == 0 and tc == 0:
                        w_ = [xtok]
                    if kc == 0 and tc > 0:
                        w_ = [("v", E.cnt["v"])]
                    E.t("matmul", out=ps_mid_t[:],
                        lhsT=xh[:, (kc * TH) + tc * 128: (kc * TH) + (tc + 1) * 128],
                        rhs=at_bf[:, kc * 64:(kc + 1) * 64],
                        start=(kc == 0), stop=(kc == 31), waits=tuple(w_),
                        skip_group_check=True,
                        incs=(("t", 1),) if kc == 31 else ())
                mtok = E.cnt["t"]
                if DEBUG_L0F == 47:
                    E.v("tensor_scalar", out=midp[:, 0:64], in0=ps_mid_t[:],
                        scalar1=1.0, scalar2=None, op0=ALU.mult,
                        waits=(("t", mtok),), incs=(("v", 1),))
                    continue
                for e in range(4):
                    E.v("tensor_scalar", out=midp[:, e * 16:(e + 1) * 16],
                        in0=ps_mid_t[:, e * 16:(e + 1) * 16],
                        scalar1=(1.0 if DEBUG_L0F == 46 else gates[:, tc * 4 + e:tc * 4 + e + 1]),
                        scalar2=None,
                        op0=ALU.mult,
                        waits=(("t", mtok), ("v", gtok)) if e == 0 else (),
                        incs=(("v", 1),) if e == 3 else ())
                ptok = E.cnt["v"]
                for e in range(4):
                    if DEBUG_L0F in (45, 46):
                        E.v("tensor_scalar",
                            out=mid_im[:, e * TH + tc * 128: e * TH + (tc + 1) * 128]
                            .rearrange("r (a b) -> r a b", a=8) if False else
                            mid_im[0:16, e * TH + tc * 128: e * TH + tc * 128 + 16],
                            in0=midp[0:16, e * 16:(e + 1) * 16],
                            scalar1=1.0, scalar2=None, op0=ALU.mult,
                            waits=(("v", ptok),) if e == 0 else (), incs=(("v", 1),))
                        continue
                    E.t("transpose", out=ps_tr[:], in_=midp[:, e * 16:(e + 1) * 16],
                        identity=D["ident"][:],
                        waits=(("v", ptok),) if e == 0 else (("v", E.cnt["v"]),),
                        incs=(("t", 1),))
                    trtok = E.cnt["t"]
                    E.v("tensor_scalar", out=mid_im[:, e * TH + tc * 128: e * TH + (tc + 1) * 128],
                        in0=ps_tr[0:16, 0:128], scalar1=1.0, scalar2=None, op0=ALU.mult,
                        waits=(("t", trtok),), incs=(("v", 1),))
            mimtok = ("v", E.cnt["v"])
            if DEBUG_L0F in (4, 45, 46, 47):
                return ident_out()
            evac_tok = {}
            for mc in range(32):
                for e in range(4):
                    w_ = []
                    if e == 0:
                        w_ = [mimtok]
                        if mc >= 2:
                            w_.append(("v", evac_tok[mc - 2]))
                    E.t("matmul", out=ps_m[mc % 2][:],
                        lhsT=bt_bf[:, e * 4096 + mc * 128: e * 4096 + (mc + 1) * 128],
                        rhs=mid_im[:, e * TH:(e + 1) * TH],
                        start=(e == 0), stop=(e == 3), waits=tuple(w_),
                        skip_group_check=True,
                        incs=(("t", 1),) if e == 3 else ())
                mmt = E.cnt["t"]
                E.v("tensor_tensor", out=s_tile[:, mc * TH:(mc + 1) * TH],
                    in0=ps_m[mc % 2][:], in1=xh[:, mc * TH:(mc + 1) * TH],
                    op=ALU.add, waits=(("t", mmt),), incs=(("v", 1),))
                evac_tok[mc] = E.cnt["v"]
            stok2 = ("v", E.cnt["v"])
            ptok2 = stats_ln(E, D, s_tile, 32, 1e-5, stok2, sq, ps1, ps2,
                             pspan, mu_pan, r_pan)
            emit_ln_out(E, D, s_tile, x_dst, h, final, mu_pan, r_pan, ptok2, fout, fout)

        for h in range(2):
            E.run_phase(phase, h)


_UID = [0]


def uid():
    _UID[0] += 1
    return _UID[0]


def ap3(ap, off, s1, n1, s2, n2):
    return bass.AP(ap.tensor, ap.offset + off, [ap.ap[0], [s1, n1], [s2, n2]])


def vwait(E):
    if E.cur == "vector":
        E.eng.wait_ge(E.sems["v"], E.cnt["v"])


def emit_proj_packed(E, D, T, PS, n_oc, words_ap, s_ap, z_ap, rhs_fn, sink_fn,
                     extra_mm_wait=None):
    w_int, nib, nib_bf, w_deq = T["w_int"], T["nib"], T["nib_bf"], T["w_deq"]
    pan_z, pan_s = T["pan_z"], T["pan_s"]
    s_bf, z_w, z_int, zbf = T["s_bf"], T["z_w"], T["z_int"], T["zbf"]
    ps_pan = PS["ps_pan"]
    ps_m = [PS["psm0"], PS["psm1"]]
    n_sup = 4
    extr_tok = {}
    pan_tok = {}
    sink_tok = {}
    mmh = {}
    cv_tok = {}
    dq_tok = {}
    for sp in range((n_oc + n_sup - 1) // n_sup):
        ocs = list(range(sp * n_sup, min((sp + 1) * n_sup, n_oc)))
        nc_ = len(ocs)
        ncw = nc_ * 16
        ncc = nc_ * 128
        spar = sp % 2
        sk = f"w{spar}"
        wt = (("s", extr_tok[sp - 2]),) if sp >= 2 else T["guard"]
        E.dma("sync", w_int[spar][:, 0:32 * ncw].rearrange("p (c w) -> p c w", c=32),
              words_ap[:, :, ocs[0] * 16: ocs[0] * 16 + ncw], waits=wt, sem=sk)
        wtok = (sk, E.cnt[sk])
        rk = f"r{spar}"
        gk = "ga" if spar == 0 else "gb"
        rt = (("v", pan_tok[sp - 2]),) if sp >= 2 else T["guard"]
        for hh2 in range(2):
            E.dma("gpsimd", s_bf[spar][hh2][0:16, 0:ncc],
                  s_ap[16 * hh2:16 * hh2 + 16, ocs[0] * 128: ocs[0] * 128 + ncc],
                  waits=rt if hh2 == 0 else (), sem=gk)
            E.dma("sync", z_w[spar][hh2][0:16, 0:ncw],
                  z_ap[16 * hh2:16 * hh2 + 16, ocs[0] * 16: ocs[0] * 16 + ncw], sem=rk)
        rtok = (rk, E.cnt[rk])
        gtok = (gk, E.cnt[gk])
        for hh2 in range(2):
            zw = z_w[spar][hh2]
            for j in range(8):
                E.v("tensor_scalar",
                    out=ap3(z_int[0:16, :], 16 * j, 128, nc_, 1, 16),
                    in0=ap3(zw[0:16, :], 0, 16, nc_, 1, 16),
                    scalar1=4 * ORDER8[j], scalar2=15,
                    op0=ALU.logical_shift_right, op1=ALU.bitwise_and,
                    waits=(rtok,) if (j == 0 and hh2 == 0) else (),
                    incs=(("v", 1),) if j == 7 else ())
            vwait(E)
            E.v("tensor_scalar", out=zbf[0:16, 0:ncc], in0=z_int[0:16, 0:ncc],
                scalar1=1.0, scalar2=None, op0=ALU.mult, incs=(("v", 1),))
            E.t("matmul", out=ps_pan[:, 0:ncc], lhsT=D["r16"][:],
                rhs=zbf[0:16, 0:ncc], start=True, stop=True,
                waits=(("v", E.cnt["v"]), ("c", 64)), skip_group_check=True,
                incs=(("t", 1),))
            E.v("tensor_scalar", out=pan_z[hh2][:, 0:ncc], in0=ps_pan[:, 0:ncc],
                scalar1=1.0, scalar2=None, op0=ALU.mult,
                waits=(("t", E.cnt["t"]),), incs=(("v", 1),))
            E.t("matmul", out=ps_pan[:, 0:ncc], lhsT=D["r16"][:],
                rhs=s_bf[spar][hh2][0:16, 0:ncc], start=True, stop=True,
                waits=(("v", E.cnt["v"]), gtok), skip_group_check=True,
                incs=(("t", 1),))
            E.v("tensor_scalar", out=pan_s[hh2][:, 0:ncc], in0=ps_pan[:, 0:ncc],
                scalar1=1.0, scalar2=None, op0=ALU.mult,
                waits=(("t", E.cnt["t"]),), incs=(("v", 1),))
            vwait(E)
        pan_tok[sp] = E.cnt["v"]
        for oi, oc in enumerate(ocs):
            for hh2 in range(2):
                hid = oc * 2 + hh2
                dpar = hid % 2
                for j in range(8):
                    w_ = [wtok] if (j == 0 and oi == 0 and hh2 == 0) else []
                    if j == 0 and cv_tok.get(hid - 2):
                        w_.append(("s", cv_tok[hid - 2]))
                    E.v("tensor_scalar",
                        out=ap3(nib[dpar][:], 16 * j, 128, 16, 1, 16),
                        in0=ap3(w_int[spar][:], hh2 * 16 * ncw + oi * 16, ncw, 16, 1, 16),
                        scalar1=4 * ORDER8[j], scalar2=15,
                        op0=ALU.logical_shift_right, op1=ALU.bitwise_and,
                        waits=tuple(w_), incs=(("v", 1),) if j == 7 else ())
                cw = [("v", E.cnt["v"])]
                if dq_tok.get(hid - 2):
                    cw.append(("v", dq_tok[hid - 2]))
                E.s("activation", out=nib_bf[dpar][:], in_=nib[dpar][:], func=AF.Copy,
                    waits=tuple(cw), incs=(("s", 1),))
                cv_tok[hid] = E.cnt["s"]
                dq_w = [("s", cv_tok[hid])]
                if hid >= 2:
                    dq_w.append(("t", mmh[hid - 2]))
                E.v("tensor_tensor", out=ap3(w_deq[dpar][:], 0, 128, 16, 1, 128),
                    in0=ap3(nib_bf[dpar][:], 0, 128, 16, 1, 128),
                    in1=ap3(pan_z[hh2][:], oi * 128, 0, 16, 1, 128),
                    op=ALU.subtract, waits=tuple(dq_w), incs=(("v", 1),))
                dq_tok[hid] = E.cnt["v"]
                E.v("tensor_tensor", out=ap3(w_deq[dpar][:], 0, 128, 16, 1, 128),
                    in0=ap3(w_deq[dpar][:], 0, 128, 16, 1, 128),
                    in1=ap3(pan_s[hh2][:], oi * 128, 0, 16, 1, 128),
                    op=ALU.mult, incs=(("v", 1),))
                dq = E.cnt["v"]
                for k2 in range(16):
                    kc = hh2 * 16 + k2
                    w_ = []
                    if k2 == 0:
                        w_.append(("v", dq))
                        if hh2 == 0:
                            war = sink_tok.get(oc - 2) or sink_tok.get(oc - 1)
                            if oc >= 1 and war:
                                w_.append(war)
                            if oc == 0 and extra_mm_wait is not None:
                                w_.append(extra_mm_wait)
                    E.t("matmul", out=ps_m[oc % 2][:],
                        lhsT=w_deq[dpar][:, k2 * 128:(k2 + 1) * 128],
                        rhs=rhs_fn(kc), start=(kc == 0), stop=(kc == 31),
                        waits=tuple(w_), skip_group_check=True, incs=(("t", 1),))
                mmh[hid] = E.cnt["t"]
            T["mm_done"] = mmh[oc * 2 + 1]
            sink_tok[oc] = sink_fn(E, oc, ps_m[oc % 2])
        extr_tok[sp] = E.cnt["s"]


DEBUG_L1A = None


def dump_dbg2(E, D, tile, off=0):
    d2 = D["dbg2"].rearrange("(c p) t -> p c t", p=128)
    E.dma("gpsimd", d2,
          tile[:, off:off + 32 * TH].rearrange("p (c t) -> p c t", c=32),
          waits=(("v", E.cnt["v"]), ("t", E.cnt["t"]), ("s", E.cnt["s"])), sem="o0")


def emit_l1a(E, nc, D, x_src, x_dst, f32_src, final):
    assert not f32_src
    x_h = D[x_src].rearrange("(c p) t -> p c t", p=128)
    xr_h = D["xr_buf"].rearrange("(c p) t -> p c t", p=128)
    qkv_w = D["qkv_qw"].rearrange("(c p) w -> p c w", p=128)
    o_w = D["o_qw"].rearrange("(c p) w -> p c w", p=128)
    gu_w = D["gu_qw"].rearrange("(c p) w -> p c w", p=128)
    A0 = D["A0"]

    u0 = uid()
    with contextlib.ExitStack() as ph:
        sb = lambda n_, sh, dt: ph.enter_context(nc.sbuf_tensor(f"{n_}_{u0}", list(sh), dt))
        hh = sb("hh", (128, 32 * TH), BF16)
        lnw = [sb("lnw1", (128, 32), F32), sb("lnw2", (128, 32), F32)]
        sq = [sb(f"sq{i}", (128, TH), BF16) for i in range(2)]
        r_pan = sb("r_pan", (128, TH), BF16)
        mu_pan = sb("mu_pan", (128, TH), BF16)
        fout = [sb(f"fo{i}", (128, TH), F32 if final else BF16) for i in range(2)]
        T = dict()
        TT = dict()
        PS = {}

        lnw_tok = [None]

        def ph_lnw(E):
            E.dma("sync", lnw[0][:], D["ln1_w"], waits=E.guard(), sem="r0")
            E.dma("sync", lnw[1][:], D["ln2_w"], sem="r0")
            lnw_tok[0] = ("r0", E.cnt["r0"])
        E.run_phase(ph_lnw)

        def rms_and_norm(E, h, src_tile, dst_tile, lnw_t, from_hbm):
            guard = T["guard"]
            mt, xt2 = {}, {}
            for kc in range(32):
                par = kc % 2
                if from_hbm:
                    w_ = (("s", xt2[kc - 2]),) if kc >= 2 else guard
                    E.dma("scalar", TT["xc"][par][:], x_h[:, kc, h * TH:(h + 1) * TH],
                          waits=w_, sem=f"x{par}")
                    srcap = TT["xc"][par][:]
                    sqw = [(f"x{par}", E.cnt[f"x{par}"])]
                else:
                    srcap = src_tile[:, kc * TH:(kc + 1) * TH]
                    sqw = list(guard) if kc == 0 else []
                if kc >= 2:
                    sqw.append(("t", mt[kc - 2]))
                E.s("activation", out=sq[par][:], in_=srcap, func=AF.Square,
                    waits=tuple(sqw), incs=(("s", 1),))
                xt2[kc] = E.cnt["s"]
                E.t("matmul", out=PS["ps2"][0:1, :], lhsT=D["ones128"][:], rhs=sq[par][:],
                    start=(kc == 0), stop=(kc == 31),
                    waits=(("s", E.cnt["s"]), ("c", 64)), skip_group_check=True,
                    incs=(("t", 1),))
                mt[kc] = E.cnt["t"]
            ttok = E.cnt["t"]
            rows = D["rows"]
            R = lambda i: rows[0:1, i * TH:(i + 1) * TH]
            E.v("tensor_scalar", out=R(0), in0=PS["ps2"][0:1, :], scalar1=1.0 / 4096,
                scalar2=None, op0=ALU.mult, waits=(("t", ttok),), incs=(("v", 1),))
            vwait(E)
            E.s("activation", out=R(0), in_=R(0), func=AF.Ln,
                bias=D["ceps"][0:1, 1:2], waits=(("v", E.cnt["v"]),))
            E.s("activation", out=R(0), in_=R(0), func=AF.Exp, scale=-0.5,
                incs=(("s", 1),))
            E.v("tensor_scalar", out=D["rows_bf"][0:1, 0:TH], in0=R(0), scalar1=1.0,
                scalar2=None, op0=ALU.mult, waits=(("s", E.cnt["s"]),), incs=(("v", 1),))
            E.t("matmul", out=PS["pspan"][:], lhsT=D["ones1"][:],
                rhs=D["rows_bf"][0:1, 0:TH], start=True, stop=True,
                waits=(("v", E.cnt["v"]),), skip_group_check=True, incs=(("t", 1),))
            E.v("tensor_scalar", out=r_pan[:], in0=PS["pspan"][:], scalar1=1.0,
                scalar2=None, op0=ALU.mult, waits=(("t", E.cnt["t"]),), incs=(("v", 1),))
            vwait(E)
            xt = {}
            for kc in range(32):
                par = kc % 2
                mw = []
                if from_hbm:
                    w_ = (("v", xt[kc - 2]),) if kc >= 2 else ()
                    E.dma("scalar", TT["xc"][par][:], x_h[:, kc, h * TH:(h + 1) * TH],
                          waits=w_, sem=f"x{par}")
                    srcap = TT["xc"][par][:]
                    mw = [(f"x{par}", E.cnt[f"x{par}"])]
                else:
                    srcap = src_tile[:, kc * TH:(kc + 1) * TH]
                if kc == 0:
                    mw.append(lnw_tok[0])
                E.v("tensor_mul", out=dst_tile[:, kc * TH:(kc + 1) * TH],
                    in0=srcap, in1=r_pan[:], waits=tuple(mw), incs=(("v", 1),))
                vwait(E)
                E.v("tensor_scalar", out=dst_tile[:, kc * TH:(kc + 1) * TH],
                    in0=dst_tile[:, kc * TH:(kc + 1) * TH],
                    scalar1=lnw_t[:, kc:kc + 1], scalar2=None, op0=ALU.mult,
                    incs=(("v", 1),))
                xt[kc] = E.cnt["v"]

        def phA(E, h):
            T["guard"] = E.guard()
            rms_and_norm(E, h, None, hh, lnw[0], True)
            h1_tok = ("v", E.cnt["v"])
            if h == 0 and DEBUG_L1A == "h1":
                dump_dbg2(E, D, hh)

            def sink_q(E, oc, ps):
                E.v("tensor_scalar", out=A0[:, oc * TH:(oc + 1) * TH], in0=ps[:],
                    scalar1=1.0, scalar2=None, op0=ALU.mult,
                    waits=(("t", T["mm_done"]),), incs=(("v", 1),))
                return ("v", E.cnt["v"])
            emit_proj_packed(E, D, T, PS, 32, qkv_w, D["qkv_sc"], D["qkv_qz"],
                             lambda kc: hh[:, kc * TH:(kc + 1) * TH], sink_q,
                             extra_mm_wait=h1_tok)
            if h == 0 and DEBUG_L1A == "q":
                dump_dbg2(E, D, A0)

        def phB(E, h):
            T["guard"] = E.guard()
            guard = T["guard"]
            xro = {}

            def sink_o(E, mc, ps):
                par = mc % 2
                w_ = ((f"o{par}", xro[mc - 2]),) if mc >= 2 else guard
                E.dma("scalar", TT["xc"][par][:], x_h[:, mc, h * TH:(h + 1) * TH],
                      waits=w_, sem=f"x{par}")
                E.v("tensor_tensor", out=hh[:, mc * TH:(mc + 1) * TH], in0=ps[:],
                    in1=TT["xc"][par][:], op=ALU.add,
                    waits=(("t", T["mm_done"]), (f"x{par}", E.cnt[f"x{par}"])),
                    incs=(("v", 1),))
                vwait(E)
                E.v("tensor_tensor", out=TT["xc"][par][:], in0=TT["xc"][par][:],
                    in1=hh[:, mc * TH:(mc + 1) * TH], op=ALU.add, incs=(("v", 1),))
                tok = ("v", E.cnt["v"])
                E.dma("gpsimd", xr_h[:, mc, :], TT["xc"][par][:],
                      waits=(("v", E.cnt["v"]),), sem=f"o{par}")
                xro[mc] = E.cnt[f"o{par}"]
                return tok
            emit_proj_packed(E, D, T, PS, 32, o_w, D["o_sc"], D["o_qz"],
                             lambda kc: A0[:, kc * TH:(kc + 1) * TH], sink_o)
            if h == 0 and DEBUG_L1A == "r2":
                dump_dbg2(E, D, hh)

        def phC(E, h):
            T["guard"] = E.guard()
            if h == 0 and DEBUG_L1A == "xr":
                E.dma("gpsimd", D["dbg2"], D["xr_buf"], waits=T["guard"], sem="o0")
            rms_and_norm(E, h, hh, hh, lnw[1], False)
            if h == 0 and DEBUG_L1A == "h2":
                dump_dbg2(E, D, hh)

        def phD(E, h):
            T["guard"] = E.guard()

            def sink_gu(E, oc, ps):
                ci = oc // 2
                if oc % 2 == 0:
                    return None
                E.s("activation", out=TT["silu_t"][:], in_=PS["psm0"][:], func=AF.Silu,
                    waits=(("t", T["mm_done"]),), incs=(("s", 1),))
                E.v("tensor_tensor", out=A0[:, ci * TH:(ci + 1) * TH], in0=ps[:],
                    in1=TT["silu_t"][:], op=ALU.mult,
                    waits=(("s", E.cnt["s"]),), incs=(("v", 1),))
                return ("v", E.cnt["v"])
            emit_proj_packed(E, D, T, PS, 172, gu_w, D["gu_sc"], D["gu_qz"],
                             lambda kc: hh[:, kc * TH:(kc + 1) * TH], sink_gu)
            if h == 0 and DEBUG_L1A == "a0":
                dump_dbg2(E, D, A0)
            if h == 0 and DEBUG_L1A == "a40":
                dump_dbg2(E, D, A0, off=40 * TH)

        def phE(E, h):
            T["guard"] = E.guard()
            emit_dn(E, nc, D, T, h, x_dst, final, PS,
                    dn_w=D["dn_qw"].rearrange("(c p) w -> p c w", p=128),
                    dn_s=D["dn_sc"], dn_z=D["dn_qz"],
                    res_h=xr_h, res_half=False,
                    s_tile=hh, sq=sq, r_pan=r_pan, mu_pan=mu_pan, fout=fout,
                    a_tile=A0, packed=True)

        for h in range(2):
            with contextlib.ExitStack() as pp:
                u2 = uid()
                sb2 = lambda n_, sh, dt: pp.enter_context(
                    nc.sbuf_tensor(f"{n_}_{u2}", list(sh), dt))
                TT.clear()
                TT["xc"] = [sb2(f"xct{i}", (128, TH), BF16) for i in range(2)]
                TT["w_int"] = [sb2(f"wi{i}", (128, 2048), I32) for i in range(2)]
                TT["nib"] = [sb2(f"nib{i}", (128, 2048), I32) for i in range(2)]
                TT["nib_bf"] = [sb2(f"nibbf{i}", (128, 2048), BF16) for i in range(2)]
                TT["w_deq"] = [sb2(f"wd{i}", (128, 2048), BF16) for i in range(2)]
                TT["s_bf"] = [[sb2(f"sbf{i}{j}", (16, 512), BF16) for j in range(2)]
                              for i in range(2)]
                TT["z_w"] = [[sb2(f"zwt{i}{j}", (16, 64), I32) for j in range(2)]
                             for i in range(2)]
                TT["z_int"] = sb2("z_intt", (16, 512), I32)
                TT["zbf"] = sb2("zbft", (16, 512), BF16)
                TT["pan_z"] = [sb2(f"pzt{i}", (128, 512), BF16) for i in range(2)]
                TT["pan_s"] = [sb2(f"pnst{i}", (128, 512), BF16) for i in range(2)]
                TT["silu_t"] = sb2("silu_tt", (128, TH), BF16)
                T.update(TT)
                PS.clear()
                PS["psm0"] = pp.enter_context(nc.psum_tensor(f"psm0_{uid()}", [128, TH], F32))
                PS["psm1"] = pp.enter_context(nc.psum_tensor(f"psm1_{uid()}", [128, TH], F32))
                PS["ps_pan"] = pp.enter_context(nc.psum_tensor(f"ps_pan_{uid()}", [128, TH], F32))
                PS["ps2"] = pp.enter_context(nc.psum_tensor(f"ps2_{uid()}", [1, TH], F32))
                PS["pspan"] = pp.enter_context(nc.psum_tensor(f"pspan_{uid()}", [128, TH], F32))
                E.run_phase(phA, h)
                E.run_phase(phB, h)
                E.run_phase(phC, h)
                E.run_phase(phD, h)
            with contextlib.ExitStack() as pp:
                PS.clear()
                for i in range(8):
                    PS[f"ps8_{i}"] = pp.enter_context(
                        nc.psum_tensor(f"ps8x{i}_{uid()}", [128, TH], F32))
                T["dnT"] = alloc_dn_tiles(nc, pp, True)
                E.run_phase(phE, h)


DN_SUPERS = [(0, 7), (7, 7), (14, 7), (21, 7), (28, 4)]


def alloc_dn_tiles(nc, pp, packed):
    u = uid()
    sb = lambda n_, sh, dt: pp.enter_context(nc.sbuf_tensor(f"{n_}_{u}", list(sh), dt))
    dnT = {}
    if packed:
        dnT["dw"] = [sb(f"dnw{i}", (128, 128), I32) for i in range(2)]
        dnT["dnib"] = [sb(f"dnib{i}", (128, 1024), I32) for i in range(2)]
        dnT["zrow"] = sb("zrow", (16, 128), I32)
        dnT["zint"] = sb("zint", (16, 1024), I32)
    else:
        dnT["dw"] = [sb(f"dnw{i}", (128, 1024), I32) for i in range(2)]
        dnT["dnib"] = None
        dnT["zrow"] = sb("zrow", (16, 1024), F32)
        dnT["zint"] = None
    dnT["ddeq"] = [sb(f"dnd{i}", (128, 1024), BF16) for i in range(2)]
    dnT["srow"] = sb("srow", (16, 1024), F32)
    dnT["szbf"] = sb("szbf", (16, 2048), BF16)
    dnT["dpz"] = [sb(f"dpz{b}", (128, 1024), BF16) for b in range(6)]
    dnT["dps"] = [sb(f"dpsn{b}", (128, 1024), BF16) for b in range(6)]
    dnT["xr_t"] = [sb(f"xrt{i}", (128, TH), BF16) for i in range(2)]
    return dnT


def emit_dn(E, nc, D, T, h, x_dst, final, PS, dn_w, dn_s, dn_z, res_h, res_half,
            s_tile, sq, r_pan, mu_pan, fout, a_tile, packed, CI=86):
    ps8 = [PS[f"ps8_{i}"] for i in range(8)]
    dnT = T["dnT"]
    dw, dnib, ddeq = dnT["dw"], dnT["dnib"], dnT["ddeq"]
    srow, zrow, zint, szbf = dnT["srow"], dnT["zrow"], dnT["zint"], dnT["szbf"]
    dpz, dps, xr_t = dnT["dpz"], dnT["dps"], dnT["xr_t"]
    if True:
        guard = T["guard"]
        t0 = E.cnt["t"]
        sink_v = {}
        extr_of = {}

        def build_panel(E, rows0, nrows, lhsT, cols, slot, w_first, mw=1024):
            rk = f"r{slot % 2}"
            E.dma("sync", srow[0:nrows, 0:mw], dn_s[rows0:rows0 + nrows, cols],
                  waits=w_first, sem=rk)
            if packed:
                E.dma("sync", zrow[0:nrows, 0:mw // 8],
                      dn_z[rows0:rows0 + nrows, cols.start // 8:cols.start // 8 + mw // 8],
                      sem=rk)
            else:
                E.dma("sync", zrow[0:nrows, 0:mw], dn_z[rows0:rows0 + nrows, cols],
                      sem=rk)
            rtok = (rk, E.cnt[rk])
            if packed:
                for j in range(8):
                    E.v("tensor_scalar",
                        out=ap3(zint[0:nrows, :], 16 * j, 128, mw // 128, 1, 16),
                        in0=ap3(zrow[0:nrows, :], 0, 16, mw // 128, 1, 16),
                        scalar1=4 * ORDER8[j], scalar2=15,
                        op0=ALU.logical_shift_right, op1=ALU.bitwise_and,
                        waits=(rtok,) if j == 0 else (),
                        incs=(("v", 1),) if j == 7 else ())
                vwait(E)
                E.v("tensor_scalar", out=szbf[0:nrows, 0:mw], in0=zint[0:nrows, 0:mw],
                    scalar1=1.0, scalar2=None, op0=ALU.mult, incs=(("v", 1),))
            else:
                E.v("tensor_scalar", out=szbf[0:nrows, 0:mw], in0=zrow[0:nrows, 0:mw],
                    scalar1=1.0, scalar2=None, op0=ALU.mult,
                    waits=(rtok,), incs=(("v", 1),))
            E.v("tensor_scalar", out=szbf[0:nrows, 1024:1024 + mw],
                in0=srow[0:nrows, 0:mw],
                scalar1=1.0, scalar2=None, op0=ALU.mult, incs=(("v", 1),))
            for b0 in range(0, mw, 512):
                bn = min(512, mw - b0)
                E.t("matmul", out=ps8[7][:, 0:bn], lhsT=lhsT,
                    rhs=szbf[0:nrows, b0:b0 + bn], start=True, stop=True,
                    waits=(("v", E.cnt["v"]), ("c", 64)), skip_group_check=True,
                    incs=(("t", 1),))
                E.v("tensor_scalar", out=dpz[slot][:, b0:b0 + bn], in0=ps8[7][:, 0:bn],
                    scalar1=1.0, scalar2=None, op0=ALU.mult,
                    waits=(("t", E.cnt["t"]),), incs=(("v", 1),))
                E.t("matmul", out=ps8[7][:, 0:bn], lhsT=lhsT,
                    rhs=szbf[0:nrows, 1024 + b0:1024 + b0 + bn], start=True, stop=True,
                    waits=(("v", E.cnt["v"]),), skip_group_check=True, incs=(("t", 1),))
                E.v("tensor_scalar", out=dps[slot][:, b0:b0 + bn], in0=ps8[7][:, 0:bn],
                    scalar1=1.0, scalar2=None, op0=ALU.mult,
                    waits=(("t", E.cnt["t"]),), incs=(("v", 1),))
            vwait(E)
            return ("v", E.cnt["v"])

        for si, (mc0, nmc) in enumerate(DN_SUPERS):
            mw = nmc * 128
            cols = slice(mc0 * 128, mc0 * 128 + mw)
            deq_done = {}
            mm_of = {}
            extr_of.clear()
            ptoks = {}
            scr_free = (("v", sink_v[si - 1]),) if si >= 1 else guard
            nat_slot = {ci: 5 for ci in range(80, CI)}
            for blk in range(5):
                w_first = scr_free if blk == 0 else (("v", ptoks[blk - 1][1]),)
                ptoks[blk] = build_panel(E, blk * 16, 16, D["r16"][:], cols, blk,
                                         w_first, mw)
            pan_all = ("v", E.cnt["v"])
            for ci in range(0, 80):
                blk = ci // 16
                emit_dn_chunk(E, D, dw, dnib, ddeq, dpz[blk], dps[blk], dn_w, ci,
                              cols, a_tile, ps8, deq_done, mm_of, extr_of, guard,
                              packed, CI, mw,
                              extra=([pan_all] if ci == 0 else [])
                              + ([("v", sink_v[si - 1]), ("t", t0)] if (ci == 0 and si >= 1) else [])
                              + ([("t", t0)] if (ci == 0 and si == 0) else []))
                if h == 0 and si == 0 and ci == 0 and DEBUG_L1A == "ddeq":
                    E.dma("gpsimd", D["dbg2"].rearrange("(a b) t -> a (b t)", b=2)[0:128, 0:mw],
                          ddeq[0][:, 0:mw], waits=(("v", deq_done[0]),), sem="o0")
            for ci in range(80, CI):
                slot = nat_slot[ci]
                w_first = (("v", deq_done[ci - 1]),) if ci >= 81 else (("v", E.cnt["v"]),)
                ptk = build_panel(E, ci, 1, D["ones1"][:], cols, slot, w_first, mw)
                emit_dn_chunk(E, D, dw, dnib, ddeq, dpz[slot], dps[slot], dn_w, ci,
                              cols, a_tile, ps8, deq_done, mm_of, extr_of, guard,
                              packed, CI, mw, extra=[ptk])
            mmt = E.cnt["t"]
            tt_tok = {}
            for m2 in range(nmc):
                mc = mc0 + m2
                par = mc % 2
                rsrc = res_h[:, mc, h * TH:(h + 1) * TH] if res_half else res_h[:, mc, :]
                if m2 >= 2:
                    w_ = (("v", tt_tok[m2 - 2]),)
                elif si >= 1:
                    w_ = (("v", sink_v[si - 1]),)
                else:
                    w_ = ()
                E.dma("scalar", xr_t[par][:], rsrc, waits=w_, sem=f"x{par}")
                E.v("tensor_tensor", out=s_tile[:, mc * TH:(mc + 1) * TH],
                    in0=ps8[m2][:], in1=xr_t[par][:], op=ALU.add,
                    waits=(("t", mmt), (f"x{par}", E.cnt[f"x{par}"])),
                    incs=(("v", 1),))
                tt_tok[m2] = E.cnt["v"]
            sink_v[si] = E.cnt["v"]
        if h == 0 and DEBUG_L1A == "s_pre":
            dump_dbg2(E, D, s_tile)
        stok = ("v", E.cnt["v"])
        ptok = stats_ln(E, D, s_tile, 32, 1e-5, stok, sq,
                        ps8[0][0:1, :], ps8[1][0:1, :], ps8[2][:],
                        mu_pan, r_pan)
        emit_ln_out(E, D, s_tile, x_dst, h, final, mu_pan, r_pan, ptok, fout, fout)


def emit_dn_chunk(E, D, dw, dnib, ddeq, pz, ps_, dn_w, ci, cols, a_tile, ps8,
                  deq_done, mm_of, extr_of, guard, packed, CI, mw, extra=()):
    par = ci % 2
    sk = f"w{par}"
    if packed:
        w_ = (("v", extr_of[ci - 2]),) if extr_of.get(ci - 2) else guard
        E.dma("sync", dw[par][:, 0:mw // 8],
              dn_w[:, ci, cols.start // 8: cols.start // 8 + mw // 8],
              waits=w_, sem=sk)
    else:
        w_ = (("v", deq_done[ci - 2]),) if deq_done.get(ci - 2) else guard
        E.dma("sync", dw[par][:, 0:mw], dn_w[:, ci, cols], waits=w_, sem=sk)
    wtok = (sk, E.cnt[sk])
    dpr = ci % 2
    if packed:
        for j in range(8):
            w_ = [wtok] if j == 0 else []
            if j == 0 and deq_done.get(ci - 2):
                w_.append(("v", deq_done[ci - 2]))
            E.v("tensor_scalar",
                out=ap3(dnib[par][:], 16 * j, 128, mw // 128, 1, 16),
                in0=ap3(dw[par][:], 0, 16, mw // 128, 1, 16),
                scalar1=4 * ORDER8[j], scalar2=15,
                op0=ALU.logical_shift_right, op1=ALU.bitwise_and,
                waits=tuple(w_), incs=(("v", 1),) if j == 7 else ())
        src_nib = dnib[par]
        extr_of[ci] = E.cnt["v"]
        mmw0 = [("v", E.cnt["v"])]
    else:
        src_nib = dw[par]
        mmw0 = [wtok]
    mmw0 += list(extra)
    if ci >= 2 and mm_of.get(ci - 2):
        mmw0.append(("t", mm_of[ci - 2]))
    E.v("tensor_tensor", out=ddeq[dpr][:, 0:mw], in0=src_nib[:, 0:mw],
        in1=pz[:, 0:mw], op=ALU.subtract, waits=tuple(mmw0), incs=(("v", 1),))
    vwait(E)
    E.v("tensor_tensor", out=ddeq[dpr][:, 0:mw], in0=ddeq[dpr][:, 0:mw],
        in1=ps_[:, 0:mw], op=ALU.mult, incs=(("v", 1),))
    dq = E.cnt["v"]
    deq_done[ci] = dq
    for m2 in range(mw // 128):
        w_ = [("v", dq)] if m2 == 0 else []
        E.t("matmul", out=ps8[m2][:],
            lhsT=ddeq[dpr][:, m2 * 128:(m2 + 1) * 128],
            rhs=a_tile[:, ci * TH:(ci + 1) * TH],
            start=(ci == 0), stop=(ci == CI - 1),
            waits=tuple(w_), skip_group_check=True, incs=(("t", 1),))
    mm_of[ci] = E.cnt["t"]


def emit_l1f(E, nc, D, x_src, x_dst, f32_src, final):
    assert not f32_src
    x_h = D[x_src].rearrange("(c p) t -> p c t", p=128)
    qg = D["f_qg"].rearrange("(c p) i -> p c i", p=128)
    qu = D["f_qu"].rearrange("(c p) i -> p c i", p=128)
    A0 = D["A0"]

    u0 = uid()
    with contextlib.ExitStack() as ph:
        sb = lambda n_, sh, dt: ph.enter_context(nc.sbuf_tensor(f"{n_}_{u0}", list(sh), dt))
        xh = sb("xhf", (128, 32 * TH), BF16)
        sq = [sb(f"sqf{i}", (128, TH), BF16) for i in range(2)]
        r_pan = sb("r_panf", (128, TH), BF16)
        mu_pan = sb("mu_panf", (128, TH), BF16)
        fout = [sb(f"fof{i}", (128, TH), F32 if final else BF16) for i in range(2)]
        T = dict()
        TT = dict()
        PS = {}

        def phF1(E, h):
            T["guard"] = E.guard()
            guard = T["guard"]
            xtok = load_xh(E, D, xh, x_src, h, False, None)
            deq_tok = {}
            mm_tok = {}
            sink_tok = {}
            pan_tok = {}
            for ci in range(86):
                for mi, (qw_ap, s_ap, z_ap) in enumerate(
                        ((qg, D["f_sg"], D["f_zg"]), (qu, D["f_su"], D["f_zu"]))):
                    hid = 2 * ci + mi
                    par = hid % 2
                    sk = f"w{par}"
                    wt = (("v", deq_tok[hid - 2]),) if hid >= 2 else guard
                    E.dma("sync", TT["w_int"][par][:, 0:2048].rearrange("p (c i) -> p c i", c=16),
                          qw_ap[:, 0:16, ci * 128:(ci + 1) * 128], waits=wt, sem=sk)
                    E.dma("sync", TT["w_int"][par][:, 2048:4096].rearrange("p (c i) -> p c i", c=16),
                          qw_ap[:, 16:32, ci * 128:(ci + 1) * 128], sem=sk)
                    wtok = (sk, E.cnt[sk])
                    gk = "ga" if par == 0 else "gb"
                    rt = (("v", pan_tok[hid - 1]),) if hid >= 1 else guard
                    for hh2 in range(2):
                        E.dma("gpsimd", TT["s_bf"][hh2][0:16, 0:128],
                              s_ap[16 * hh2:16 * hh2 + 16, ci * 128:(ci + 1) * 128],
                              waits=rt if hh2 == 0 else (), sem=gk)
                        E.dma("gpsimd", TT["z_bf"][hh2][0:16, 0:128],
                              z_ap[16 * hh2:16 * hh2 + 16, ci * 128:(ci + 1) * 128],
                              sem=gk)
                    rtok = (gk, E.cnt[gk])
                    for hh2 in range(2):
                        E.t("matmul", out=PS["ps_pan"][:, 0:128], lhsT=D["r16"][:],
                            rhs=TT["z_bf"][hh2][0:16, 0:128], start=True, stop=True,
                            waits=(rtok, ("v", E.cnt["v"]), ("c", 64)),
                            skip_group_check=True, incs=(("t", 1),))
                        E.v("tensor_scalar", out=TT["pan_z"][hh2][:], in0=PS["ps_pan"][:, 0:128],
                            scalar1=1.0, scalar2=None, op0=ALU.mult,
                            waits=(("t", E.cnt["t"]),), incs=(("v", 1),))
                        E.t("matmul", out=PS["ps_pan"][:, 0:128], lhsT=D["r16"][:],
                            rhs=TT["s_bf"][hh2][0:16, 0:128], start=True, stop=True,
                            waits=(("v", E.cnt["v"]),), skip_group_check=True,
                            incs=(("t", 1),))
                        E.v("tensor_scalar", out=TT["pan_s"][hh2][:], in0=PS["ps_pan"][:, 0:128],
                            scalar1=1.0, scalar2=None, op0=ALU.mult,
                            waits=(("t", E.cnt["t"]),), incs=(("v", 1),))
                    vwait(E)
                    pan_tok[hid] = E.cnt["v"]
                    if h == 0 and hid == 0 and DEBUG_L1A == "fpan":
                        d2 = D["dbg2"]
                        E.dma("gpsimd", d2[0:128, 0:128], TT["pan_z"][0][:],
                              waits=(("v", E.cnt["v"]),), sem="o0")
                        E.dma("gpsimd", d2[0:128, 128:256], TT["pan_s"][0][:], sem="o0")
                        E.dma("gpsimd", d2[0:16, 256:384], TT["z_bf"][0][0:16, :], sem="o0")
                        E.dma("gpsimd", d2[0:16, 384:512], TT["s_bf"][0][0:16, :], sem="o0")
                        E.v("memset", D["vdummy"][0:1, 0:8], 0.0,
                            waits=(("o0", E.cnt["o0"]),), incs=(("v", 1),))
                    cw = [wtok]
                    if hid >= 2:
                        cw.append(("t", mm_tok[hid - 2]))
                    E.s("activation", out=TT["w_deq"][par][:], in_=TT["w_int"][par][:],
                        func=AF.Copy, waits=tuple(cw), incs=(("s", 1),))
                    cvt = E.cnt["s"]
                    for hh2 in range(2):
                        E.v("tensor_tensor",
                            out=ap3(TT["w_deq"][par][:], hh2 * 2048, 128, 16, 1, 128),
                            in0=ap3(TT["w_deq"][par][:], hh2 * 2048, 128, 16, 1, 128),
                            in1=ap3(TT["pan_z"][hh2][:], 0, 0, 16, 1, 128),
                            op=ALU.subtract,
                            waits=(("s", cvt),) if hh2 == 0 else (), incs=(("v", 1),))
                        E.v("tensor_tensor",
                            out=ap3(TT["w_deq"][par][:], hh2 * 2048, 128, 16, 1, 128),
                            in0=ap3(TT["w_deq"][par][:], hh2 * 2048, 128, 16, 1, 128),
                            in1=ap3(TT["pan_s"][hh2][:], 0, 0, 16, 1, 128),
                            op=ALU.mult, incs=(("v", 1),))
                    deq_tok[hid] = E.cnt["v"]
                    ps = PS["psm0"] if mi == 0 else PS["psm1"]
                    for kc in range(32):
                        w_ = []
                        if kc == 0:
                            w_.append(("v", deq_tok[hid]))
                            if mi == 0:
                                war = sink_tok.get(ci - 1)
                                if war:
                                    w_.append(war)
                            if hid == 0:
                                w_.append(xtok)
                        E.t("matmul", out=ps[:],
                            lhsT=TT["w_deq"][par][:, kc * 128:(kc + 1) * 128],
                            rhs=xh[:, kc * TH:(kc + 1) * TH],
                            start=(kc == 0), stop=(kc == 31),
                            waits=tuple(w_), skip_group_check=True, incs=(("t", 1),))
                    mm_tok[hid] = E.cnt["t"]
                E.s("activation", out=TT["silu_t"][:], in_=PS["psm0"][:], func=AF.Silu,
                    waits=(("t", mm_tok[2 * ci + 1]),), incs=(("s", 1),))
                E.v("tensor_tensor", out=A0[:, ci * TH:(ci + 1) * TH], in0=PS["psm1"][:],
                    in1=TT["silu_t"][:], op=ALU.mult,
                    waits=(("s", E.cnt["s"]),), incs=(("v", 1),))
                sink_tok[ci] = ("v", E.cnt["v"])

        def phF2(E, h):
            T["guard"] = E.guard()
            if h == 0 and DEBUG_L1A == "fa0":
                dump_dbg2(E, D, A0)
            if h == 0 and DEBUG_L1A == "fa40":
                dump_dbg2(E, D, A0, off=40 * TH)
            emit_dn(E, nc, D, T, h, x_dst, final, PS,
                    dn_w=D["f_qd"].rearrange("(c p) m -> p c m", p=128),
                    dn_s=D["f_sd"], dn_z=D["f_zd"],
                    res_h=x_h, res_half=True,
                    s_tile=xh, sq=sq, r_pan=r_pan, mu_pan=mu_pan, fout=fout,
                    a_tile=A0, packed=False)

        for h in range(2):
            with contextlib.ExitStack() as pp:
                u2 = uid()
                sb2 = lambda n_, sh, dt: pp.enter_context(
                    nc.sbuf_tensor(f"{n_}_{u2}", list(sh), dt))
                TT.clear()
                TT["w_int"] = [sb2(f"wif{i}", (128, 4096), I32) for i in range(2)]
                TT["w_deq"] = [sb2(f"wdf{i}", (128, 4096), BF16) for i in range(2)]
                TT["s_bf"] = [sb2(f"sbff{j}", (16, 128), BF16) for j in range(2)]
                TT["z_bf"] = [sb2(f"zbff{j}", (16, 128), BF16) for j in range(2)]
                TT["pan_z"] = [sb2(f"pzf{i}", (128, 128), BF16) for i in range(2)]
                TT["pan_s"] = [sb2(f"pnsf{i}", (128, 128), BF16) for i in range(2)]
                TT["silu_t"] = sb2("silu_tf", (128, TH), BF16)
                PS.clear()
                PS["psm0"] = pp.enter_context(nc.psum_tensor(f"psm0_{uid()}", [128, TH], F32))
                PS["psm1"] = pp.enter_context(nc.psum_tensor(f"psm1_{uid()}", [128, TH], F32))
                PS["ps_pan"] = pp.enter_context(nc.psum_tensor(f"ps_pan_{uid()}", [128, TH], F32))
                E.run_phase(phF1, h)
            with contextlib.ExitStack() as pp:
                PS.clear()
                for i in range(8):
                    PS[f"ps8_{i}"] = pp.enter_context(
                        nc.psum_tensor(f"ps8y{i}_{uid()}", [128, TH], F32))
                T["dnT"] = alloc_dn_tiles(nc, pp, False)
                E.run_phase(phF2, h)


EMITTERS = {"l0a": emit_l0a, "l0f": emit_l0f, "l1a": emit_l1a, "l1f": emit_l1f}


SUBBLOCKS_PER_LAUNCH = 4
_launch_cache = {}


def get_launches(sub_blocks_all=None, per_launch=None):
    sub_blocks_all = list(SUBBLOCKS) if sub_blocks_all is None else sub_blocks_all
    per_launch = SUBBLOCKS_PER_LAUNCH if per_launch is None else per_launch
    key = (tuple(sub_blocks_all), per_launch)
    if key not in _launch_cache:
        groups = [sub_blocks_all[i:i + per_launch]
                  for i in range(0, len(sub_blocks_all), per_launch)]
        _launch_cache[key] = [(g, build_launch(g, gi == 0, gi == len(groups) - 1))
                              for gi, g in enumerate(groups)]
    return _launch_cache[key]


WKEYS = {
    "l0a": ["l0_w", "l0_m"],
    "l0f": ["l0f_at", "l0f_bt", "l0f_wrt"],
    "l1a": ["qkv_qw", "qkv_qz", "qkv_sc", "o_qw", "o_qz", "o_sc",
            "gu_qw", "gu_qz", "gu_sc", "dn_qw", "dn_qz", "dn_sc", "ln1_w", "ln2_w"],
    "l1f": ["f_qg", "f_qu", "f_qd", "f_sg", "f_zg", "f_su", "f_zu", "f_sd", "f_zd"],
}
CKEYS = ["c_r16", "c_ones1", "c_ones128", "c_ident"]


def run_all(S, cores, P, progs, n=8):
    x_cur = [c["x0"] for c in cores]
    first = True
    for (g, prog) in progs:
        in_maps = []
        for c in range(n):
            m = {k: S[k] for k in CKEYS}
            for sbn in g:
                for k in WKEYS[sbn]:
                    m[k] = S[k]
            m["x0" if first else "x_in"] = x_cur[c]
            in_maps.append(m)
        res = run_bass_kernel_spmd(prog, in_maps, core_ids=list(range(n)))
        x_cur = [np.asarray(res.results[c]["x_out"]) for c in range(n)]
        first = False
    return x_cur


def kernel(**inputs):
    S, cores, P, PI = prep_all(inputs)
    progs = get_launches()
    x_cur = run_all(S, cores, P, progs)
    out = np.empty((8192, 4096), np.float32)
    for c in range(8):
        out[c * TC:(c + 1) * TC][:, P] = x_cur[c].T.astype(np.float32)
    return out.reshape(2, 4096, 4096)


# version 06 (final)
# speedup vs baseline: 1.1352x; 1.1352x over previous
import contextlib
import numpy as np
import ml_dtypes
import concourse.bass as bass
import concourse.mybir as mybir
from concourse.bass_utils import run_bass_kernel_spmd

F32 = mybir.dt.float32
BF16 = mybir.dt.bfloat16
I32 = mybir.dt.int32
I16 = mybir.dt.int16
AF = mybir.ActivationFunctionType
ALU = mybir.AluOpType
AX = mybir.AxisListType

TC = 1024
TH = 512
BF = ml_dtypes.bfloat16
ORDER8 = [0, 4, 1, 5, 2, 6, 3, 7]


def _intra(p):
    u = (p % 32) // 2
    return u * 8 + 2 * (p // 32) + (p % 2), u


def _interleave_perm(n_feat):
    n_chunks = n_feat // 128
    perm = np.zeros(n_feat, dtype=np.int64)
    for c in range(n_chunks):
        for p in range(128):
            off, u = _intra(p)
            g = 16 * (c // 16) + u
            perm[c * 128 + p] = g * 128 + (c % 16) * 8 + (off % 8)
    return perm


def _interleave_perm_i(n=11008):
    perm = np.zeros(n, dtype=np.int64)
    for c in range(86):
        for p in range(128):
            off, u = _intra(p)
            if c < 80:
                g = 16 * (c // 16) + u
                perm[c * 128 + p] = g * 128 + (c % 16) * 8 + (off % 8)
            else:
                perm[c * 128 + p] = c * 128 + off
    return perm


def _word_gather_map(perm, n=None):
    n = len(perm) if n is None else n
    wm = np.zeros(n // 8, dtype=np.int64)
    for c in range(n // 128):
        for u in range(16):
            wm[c * 16 + u] = perm[c * 128 + 2 * u] // 8
    return wm


def prep_all(inputs, n_cores=8):
    P = _interleave_perm(4096)
    PI = _interleave_perm_i(11008)
    WM_P = _word_gather_map(P)
    WM_I = _word_gather_map(PI)
    S = {}
    f32 = np.float32

    w = np.asarray(inputs["L0_attn_weight"])
    m = np.asarray(inputs["L0_attn_mask"])
    S["l0_w"] = np.ascontiguousarray(w[P][:, P].T)
    S["l0_m"] = np.ascontiguousarray(m[P][:, P].T)

    A = np.asarray(inputs["L0_ffn_A"])
    B = np.asarray(inputs["L0_ffn_B"])
    Wr = np.asarray(inputs["L0_ffn_Wr"])
    at = A[:, :, P].transpose(2, 0, 1).reshape(4096, 64)
    S["l0f_at"] = np.ascontiguousarray(
        at.reshape(32, 128, 64).transpose(1, 0, 2).reshape(128, 2048)).astype(f32)
    S["l0f_bt"] = np.ascontiguousarray(
        B[:, P, :].transpose(0, 2, 1).reshape(4, 16, 4096).transpose(1, 0, 2)
        .reshape(16, 4 * 4096)).astype(f32)
    wrt = Wr[:, P].T
    S["l0f_wrt"] = np.ascontiguousarray(
        wrt.reshape(32, 128, 4).transpose(1, 0, 2).reshape(128, 128)).astype(f32)

    S["qkv_qw"] = np.ascontiguousarray(np.asarray(inputs["L1_attn_qkv_qw"])[P][:, WM_P])
    S["qkv_qz"] = np.ascontiguousarray(np.asarray(inputs["L1_attn_qkv_qz"])[:, WM_P])
    S["qkv_sc"] = np.ascontiguousarray(np.asarray(inputs["L1_attn_qkv_sc"])[:, :4096][:, P])

    S["o_qw"] = np.ascontiguousarray(np.asarray(inputs["L1_attn_o_qw"])[P][:, WM_P])
    S["o_qz"] = np.ascontiguousarray(np.asarray(inputs["L1_attn_o_qz"])[:, WM_P])
    S["o_sc"] = np.ascontiguousarray(np.asarray(inputs["L1_attn_o_sc"])[:, P])

    wm_g = WM_I.reshape(86, 16)
    wm_gu = np.stack([wm_g, 1376 + wm_g], axis=1).reshape(-1)
    cm_g = PI.reshape(86, 128)
    cm_gu = np.stack([cm_g, 11008 + cm_g], axis=1).reshape(-1)
    S["gu_qw"] = np.ascontiguousarray(np.asarray(inputs["L1_attn_gu_qw"])[P][:, wm_gu])
    S["gu_qz"] = np.ascontiguousarray(np.asarray(inputs["L1_attn_gu_qz"])[:, wm_gu])
    S["gu_sc"] = np.ascontiguousarray(np.asarray(inputs["L1_attn_gu_sc"])[:, cm_gu])

    S["dn_qw"] = np.ascontiguousarray(np.asarray(inputs["L1_attn_dn_qw"])[PI][:, WM_P])
    S["dn_qz"] = np.ascontiguousarray(np.asarray(inputs["L1_attn_dn_qz"])[:, WM_P])
    S["dn_sc"] = np.ascontiguousarray(np.asarray(inputs["L1_attn_dn_sc"])[:, P])

    S["ln1_w"] = np.ascontiguousarray(np.asarray(inputs["L1_attn_ln1_w"])[P].reshape(32, 128).T).astype(f32)
    S["ln2_w"] = np.ascontiguousarray(np.asarray(inputs["L1_attn_ln2_w"])[P].reshape(32, 128).T).astype(f32)

    S["f_qg"] = np.ascontiguousarray(np.asarray(inputs["L1_ffn_qw_gate"])[PI][:, P].T)
    S["f_qu"] = np.ascontiguousarray(np.asarray(inputs["L1_ffn_qw_up"])[PI][:, P].T)
    S["f_qd"] = np.ascontiguousarray(np.asarray(inputs["L1_ffn_qw_down"])[P][:, PI].T)
    S["f_sg"] = np.ascontiguousarray(np.asarray(inputs["L1_ffn_s_gate"])[PI].T).astype(f32)
    S["f_zg"] = np.ascontiguousarray(np.asarray(inputs["L1_ffn_z_gate"])[PI].T).astype(f32)
    S["f_su"] = np.ascontiguousarray(np.asarray(inputs["L1_ffn_s_up"])[PI].T).astype(f32)
    S["f_zu"] = np.ascontiguousarray(np.asarray(inputs["L1_ffn_z_up"])[PI].T).astype(f32)
    S["f_sd"] = np.ascontiguousarray(np.asarray(inputs["L1_ffn_s_down"])[P].T).astype(f32)
    S["f_zd"] = np.ascontiguousarray(np.asarray(inputs["L1_ffn_z_down"])[P].T).astype(f32)

    R16 = np.zeros((16, 128), dtype=BF)
    R16[(np.arange(128) % 32) // 2, np.arange(128)] = 1.0
    S["c_r16"] = R16
    S["c_ones1"] = np.ones((1, 128), dtype=BF)
    S["c_ones128"] = np.ones((128, 1), dtype=BF)
    S["c_ident"] = np.eye(128, dtype=BF)

    x0 = np.asarray(inputs["L0_attn_data"]).reshape(8192, 4096)
    x0_fm = np.ascontiguousarray(x0[:, P].T)
    cores = [{"x0": np.ascontiguousarray(x0_fm[:, c * TC:(c + 1) * TC])} for c in range(n_cores)]
    return S, cores, P, PI



ENGS = ("vector", "scalar", "tensor", "gpsimd", "sync")
GUARD_SEMS = ("v", "t", "s", "g", "o0", "o1")


class Emit:

    def __init__(self, nc, sems):
        self.nc = nc
        self.sems = sems
        self.base = {k: 0 for k in sems}
        self.cnt = dict(self.base)
        self.cur = None
        self.eng = None

    def run_phase(self, phase_fn, *args):
        snap = dict(self.base)
        with self.nc.Block() as blk:
            for name in ENGS:
                self.cur = name
                self.cnt = dict(snap)

                def body(eng, fn=phase_fn, a=args):
                    self.eng = eng
                    fn(self, *a)
                getattr(blk, name)(body)
        self.base = dict(self.cnt)

    def snap(self):
        return dict(self.cnt)

    def guard(self):
        return tuple((k, self.cnt[k]) for k in GUARD_SEMS)

    def op(self, engine, method, *args, waits=(), incs=(), **kw):
        if self.cur == engine:
            for (s, v) in waits:
                if v > 0:
                    self.eng.wait_ge(self.sems[s], v)
            ins = getattr(self.eng, method)(*args, **kw)
            for (s, n) in incs:
                ins = ins.then_inc(self.sems[s], n)
        for (s, n) in incs:
            self.cnt[s] += n

    def v(self, method, *a, **k):
        self.op("vector", method, *a, **k)

    def s(self, method, *a, **k):
        self.op("scalar", method, *a, **k)

    def t(self, method, *a, **k):
        self.op("tensor", method, *a, **k)

    def g(self, method, *a, **k):
        self.op("gpsimd", method, *a, **k)

    def dma(self, queue, out, in_, waits=(), sem="w", **kw):
        self.op(queue, "dma_start", out=out, in_=in_, waits=waits,
                incs=((sem, 16),), **kw)


def bcast_free(ap, reps):
    r = ap.rearrange("p (c f) -> p c f", c=1)
    return bass.AP(r.tensor, r.offset, [r.ap[0], [0, reps], r.ap[2]])



SUBBLOCKS = ("l0a", "l0f", "l1a", "l1f")


def build_launch(sub_blocks, is_first, is_last):
    nc = bass.Bass("TRN2", target_bir_lowering=False)
    st = contextlib.ExitStack()
    D = {}

    def din(name, shape, dtype):
        D[name] = nc.dram_tensor(name, list(shape), dtype, kind="ExternalInput").ap()

    if is_first:
        din("x0", (4096, TC), F32)
    else:
        din("x_in", (4096, TC), BF16)
    D["x_out"] = nc.dram_tensor("x_out", [4096, TC], F32 if is_last else BF16,
                                kind="ExternalOutput").ap()
    for i in range(len(sub_blocks) - 1):
        D[f"xb{i}"] = nc.dram_tensor(f"xb{i}", [4096, TC], BF16, kind="Internal").ap()
    D["xr_buf"] = nc.dram_tensor("xr_buf", [4096, TH], BF16, kind="Internal").ap()
    D["dbg"] = nc.dram_tensor("dbg", [128, 128], F32, kind="ExternalOutput").ap()
    D["dbg2"] = nc.dram_tensor("dbg2", [4096, TH], BF16, kind="ExternalOutput").ap()

    din("c_r16", (16, 128), BF16)
    din("c_ones1", (1, 128), BF16)
    din("c_ones128", (128, 1), BF16)
    din("c_ident", (128, 128), BF16)

    if "l0a" in sub_blocks:
        din("l0_w", (4096, 4096), F32)
        din("l0_m", (4096, 4096), I32)
    if "l0f" in sub_blocks:
        din("l0f_at", (128, 2048), F32)
        din("l0f_bt", (16, 4 * 4096), F32)
        din("l0f_wrt", (128, 128), F32)
    if "l1a" in sub_blocks:
        din("qkv_qw", (4096, 512), I32)
        din("qkv_qz", (32, 512), I32)
        din("qkv_sc", (32, 4096), F32)
        din("o_qw", (4096, 512), I32)
        din("o_qz", (32, 512), I32)
        din("o_sc", (32, 4096), F32)
        din("gu_qw", (4096, 2752), I32)
        din("gu_qz", (32, 2752), I32)
        din("gu_sc", (32, 22016), F32)
        din("dn_qw", (11008, 512), I32)
        din("dn_qz", (86, 512), I32)
        din("dn_sc", (86, 4096), F32)
        din("ln1_w", (128, 32), F32)
        din("ln2_w", (128, 32), F32)
    if "l1f" in sub_blocks:
        din("f_qg", (4096, 11008), I32)
        din("f_qu", (4096, 11008), I32)
        din("f_qd", (11008, 4096), I32)
        for k in ("f_sg", "f_zg", "f_su", "f_zu"):
            din(k, (32, 11008), F32)
        for k in ("f_sd", "f_zd"):
            din(k, (86, 4096), F32)

    def sb_(name, shape, dt, stack=st):
        return stack.enter_context(nc.sbuf_tensor(name, list(shape), dt))
    D["sb"] = sb_
    D["A0"] = sb_("A0", (128, 86 * TH), BF16)
    D["r16"] = sb_("r16", (16, 128), BF16)
    D["ones1"] = sb_("ones1", (1, 128), BF16)
    D["ones128"] = sb_("ones128", (128, 1), BF16)
    D["ident"] = sb_("ident", (128, 128), BF16)
    D["rows"] = sb_("rows", (1, 4 * TH), F32)
    D["ceps"] = sb_("ceps", (1, 2), F32)
    D["rows_bf"] = sb_("rows_bf", (1, 2 * TH), BF16)
    D["vdummy"] = sb_("vdummy", (1, 8), F32)

    sems = {}
    for s_ in ("c", "xh", "t", "v", "s", "g",
               "w0", "w1", "x0", "x1", "o0", "o1", "r0", "r1", "ga", "gb"):
        sems[s_] = st.enter_context(nc.semaphore("sem_" + s_))
    E = Emit(nc, sems)

    def ph_prologue(E):
        E.v("memset", D["ceps"][0:1, 0:1], 1e-5)
        E.v("memset", D["ceps"][0:1, 1:2], 1e-6, incs=(("v", 1),))
        E.dma("sync", D["r16"][:], D["c_r16"], sem="c")
        E.dma("sync", D["ones1"][:], D["c_ones1"], sem="c")
        E.dma("sync", D["ones128"][:], D["c_ones128"], sem="c")
        E.dma("sync", D["ident"][:], D["c_ident"], sem="c")
    E.run_phase(ph_prologue)
    E.ctok = ("c", 64)

    for bi, sbname in enumerate(sub_blocks):
        x_src = ("x0" if is_first else "x_in") if bi == 0 else f"xb{bi - 1}"
        x_dst = "x_out" if bi == len(sub_blocks) - 1 else f"xb{bi}"
        final = (bi == len(sub_blocks) - 1) and is_last
        EMITTERS[sbname](E, nc, D, x_src, x_dst, bi == 0 and is_first, final)

    fin = E.snap()
    dma_sems = ("c", "xh", "w0", "w1", "x0", "x1", "o0", "o1", "r0", "r1", "ga", "gb")

    def ph_epilogue(E):
        if E.cur == "sync":
            for s_ in dma_sems:
                if fin[s_] > 0:
                    E.eng.wait_ge(E.sems[s_], fin[s_])
    E.run_phase(ph_epilogue)
    st.close()
    return nc



def load_xh(E, D, xh, x_src, h, f32_src, x0_t):
    guard = E.guard()
    src = D[x_src].rearrange("(c p) t -> p c t", p=128)
    if f32_src:
        vtok = {}
        for kc in range(32):
            par = kc % 2
            w_ = guard if kc < 2 else (("v", vtok[kc - 2]),)
            sk = f"x{par}"
            E.dma("scalar", x0_t[par][:], src[:, kc, h * TH:(h + 1) * TH], waits=w_, sem=sk)
            E.v("tensor_scalar", out=xh[:, kc * TH:(kc + 1) * TH], in0=x0_t[par][:],
                scalar1=1.0, scalar2=None, op0=ALU.mult,
                waits=((sk, E.cnt[sk]),), incs=(("v", 1),))
            vtok[kc] = E.cnt["v"]
        return ("v", E.cnt["v"])
    else:
        E.dma("scalar", xh[:].rearrange("p (c t) -> p c t", c=32),
              src[:, :, h * TH:(h + 1) * TH], waits=guard, sem="xh")
        return ("xh", E.cnt["xh"])


def stats_ln(E, D, s_tile, n_chunks, eps, stok, sq, ps1, ps2, pspan,
             mu_pan, r_pan, with_mean=True, denom=4096.0):
    rows = D["rows"]
    rows_bf = D["rows_bf"]
    ttoks = {}
    for kc in range(n_chunks):
        w_ = [stok] if kc == 0 else []
        if kc >= 2:
            w_.append(("t", ttoks[kc - 2]))
        E.s("activation", out=sq[kc % 2][:], in_=s_tile[:, kc * TH:(kc + 1) * TH],
            func=AF.Square, waits=tuple(w_), incs=(("s", 1),))
        if with_mean:
            E.t("matmul", out=ps1[0:1, :], lhsT=D["ones128"][:],
                rhs=s_tile[:, kc * TH:(kc + 1) * TH],
                start=(kc == 0), stop=(kc == n_chunks - 1),
                waits=(stok, ("c", 64)) if kc == 0 else (), skip_group_check=True)
        E.t("matmul", out=ps2[0:1, :], lhsT=D["ones128"][:], rhs=sq[kc % 2][:],
            start=(kc == 0), stop=(kc == n_chunks - 1),
            waits=(("s", E.cnt["s"]),) + ((("c", 64),) if kc == 0 else ()),
            skip_group_check=True, incs=(("t", 1),))
        ttoks[kc] = E.cnt["t"]
    ttok = E.cnt["t"]
    MU, MS2, MU2, VAR = 0, 1, 2, 3
    R = lambda i: rows[0:1, i * TH:(i + 1) * TH]
    if with_mean:
        E.v("tensor_scalar", out=R(MU), in0=ps1[0:1, :], scalar1=1.0 / denom, scalar2=None,
            op0=ALU.mult, waits=(("t", ttok),))
    E.v("tensor_scalar", out=R(MS2), in0=ps2[0:1, :], scalar1=1.0 / denom, scalar2=None,
        op0=ALU.mult, waits=(("t", ttok),), incs=(("v", 1),))
    if E.cur == "vector":
        E.eng.wait_ge(E.sems["v"], E.cnt["v"])
    if with_mean:
        E.v("tensor_mul", out=R(MU2), in0=R(MU), in1=R(MU), incs=(("v", 1),))
        if E.cur == "vector":
            E.eng.wait_ge(E.sems["v"], E.cnt["v"])
        E.v("tensor_sub", out=R(VAR), in0=R(MS2), in1=R(MU2), incs=(("v", 1),))
    else:
        E.v("tensor_scalar", out=R(VAR), in0=R(MS2), scalar1=1.0, scalar2=None,
            op0=ALU.mult, incs=(("v", 1),))
    if E.cur == "vector":
        E.eng.wait_ge(E.sems["v"], E.cnt["v"])
    vtok = E.cnt["v"]
    E.s("activation", out=R(VAR), in_=R(VAR), func=AF.Ln,
        bias=D["ceps"][0:1, (0 if eps == 1e-5 else 1):(1 if eps == 1e-5 else 2)],
        waits=(("v", vtok),))
    E.s("activation", out=R(VAR), in_=R(VAR), func=AF.Exp, scale=-0.5, incs=(("s", 1),))
    stok2 = E.cnt["s"]
    E.v("tensor_scalar", out=rows_bf[0:1, 0:TH], in0=R(VAR), scalar1=1.0, scalar2=None,
        op0=ALU.mult, waits=(("s", stok2),), incs=(("v", 1),))
    if with_mean:
        E.v("tensor_scalar", out=rows_bf[0:1, TH:2 * TH], in0=R(MU), scalar1=1.0, scalar2=None,
            op0=ALU.mult, incs=(("v", 1),))
    vtok = E.cnt["v"]
    E.t("matmul", out=pspan[:], lhsT=D["ones1"][:], rhs=rows_bf[0:1, 0:TH],
        start=True, stop=True, waits=(("v", vtok),), incs=(("t", 1),))
    ttok = E.cnt["t"]
    E.v("tensor_scalar", out=r_pan[:], in0=pspan[:], scalar1=1.0, scalar2=None,
        op0=ALU.mult, waits=(("t", ttok),), incs=(("v", 1),))
    if with_mean:
        E.t("matmul", out=pspan[:], lhsT=D["ones1"][:], rhs=rows_bf[0:1, TH:2 * TH],
            start=True, stop=True, waits=(("v", E.cnt["v"]),), incs=(("t", 1),))
        ttok = E.cnt["t"]
        E.v("tensor_scalar", out=mu_pan[:], in0=pspan[:], scalar1=1.0, scalar2=None,
            op0=ALU.mult, waits=(("t", ttok),), incs=(("v", 1),))
    return ("v", E.cnt["v"])


DEBUG_SKIP_LN = False


def emit_ln_out(E, D, s_tile, x_dst, h, final, mu_pan, r_pan, ptok, fout, scratch):
    dst = D[x_dst].rearrange("(c p) t -> p c t", p=128)
    otok = {}
    if DEBUG_SKIP_LN:
        for kc in range(32):
            w_ = (ptok,) if kc == 0 else ()
            if kc >= 2:
                w_ = w_ + ((f"o{kc % 2}", otok[kc - 2]),)
            tmp = fout[kc % 2]
            E.v("tensor_scalar", out=tmp[:], in0=s_tile[:, kc * TH:(kc + 1) * TH],
                scalar1=1.0, scalar2=None, op0=ALU.mult, waits=w_, incs=(("v", 1),))
            E.dma("gpsimd", dst[:, kc, h * TH:(h + 1) * TH], tmp[:],
                  waits=(("v", E.cnt["v"]),), sem=f"o{kc % 2}")
            otok[kc] = E.cnt[f"o{kc % 2}"]
        return
    if not final:
        for kc in range(32):
            src = s_tile[:, kc * TH:(kc + 1) * TH]
            w_ = (ptok,) if kc == 0 else ()
            if kc >= 2:
                w_ = w_ + ((f"o{kc % 2}", otok[kc - 2]),)
            tmp = scratch[kc % 2]
            E.v("tensor_sub", out=tmp[:], in0=src, in1=mu_pan[:], waits=w_)
            E.v("tensor_mul", out=tmp[:], in0=tmp[:], in1=r_pan[:], incs=(("v", 1),))
            E.dma("gpsimd", dst[:, kc, h * TH:(h + 1) * TH], tmp[:],
                  waits=(("v", E.cnt["v"]),), sem=f"o{kc % 2}")
            otok[kc] = E.cnt[f"o{kc % 2}"]
    else:
        for kc in range(32):
            src = s_tile[:, kc * TH:(kc + 1) * TH]
            w_ = (ptok,) if kc == 0 else ()
            if kc >= 2:
                w_ = w_ + ((f"o{kc % 2}", otok[kc - 2]),)
            tmp = fout[kc % 2]
            E.v("tensor_sub", out=tmp[:], in0=src, in1=mu_pan[:], waits=w_)
            E.v("tensor_mul", out=tmp[:], in0=tmp[:], in1=r_pan[:], incs=(("v", 1),))
            E.dma("gpsimd", dst[:, kc, h * TH:(h + 1) * TH], tmp[:],
                  waits=(("v", E.cnt["v"]),), sem=f"o{kc % 2}")
            otok[kc] = E.cnt[f"o{kc % 2}"]



def emit_l0a(E, nc, D, x_src, x_dst, f32_src, final):
    u0 = uid()
    with contextlib.ExitStack() as ph:
        sb = lambda n_, sh, dt: ph.enter_context(nc.sbuf_tensor(f"{n_}_{u0}", list(sh), dt))
        xh = sb("xh", (128, 32 * TH), BF16)
        x0_t = [sb(f"x0t{i}", (128, TH), F32) for i in range(2)] if f32_src else None
        wbuf = [sb(f"l0w{i}", (128, 16 * 128), F32) for i in range(2)]
        mbuf = [sb(f"l0m{i}", (128, 16 * 128), I32) for i in range(2)]
        wd = [sb(f"l0d{i}", (128, 16 * 128), BF16) for i in range(2)]
        sq = [sb(f"sq{i}", (128, TH), BF16) for i in range(2)]
        mu_pan = sb("mu_pan", (128, TH), BF16)
        r_pan = sb("r_pan", (128, TH), BF16)
        fout = [sb(f"fo{i}", (128, TH), F32 if final else BF16) for i in range(2)]
        ps_m = [ph.enter_context(nc.psum_tensor(f"psm{i}_{uid()}", [128, TH], F32)) for i in range(2)]
        ps1 = ph.enter_context(nc.psum_tensor(f"ps1_{uid()}", [1, TH], F32))
        ps2 = ph.enter_context(nc.psum_tensor(f"ps2_{uid()}", [1, TH], F32))
        pspan = ph.enter_context(nc.psum_tensor(f"pspan_{uid()}", [128, TH], F32))

        w_h = D["l0_w"].rearrange("(kc p) n -> p kc n", p=128)
        m_h = D["l0_m"].rearrange("(kc p) n -> p kc n", p=128)
        s_tile = D["A0"][:, 0:32 * TH]

        def phase(E, h):
            guard = E.guard()
            xtok = load_xh(E, D, xh, x_src, h, f32_src, x0_t)
            deq_tok = {}
            evac_tok = {}
            grp_tok = {}
            bi = 0
            for nc_ in range(32):
                for hh in range(2):
                    par = bi % 2
                    wt = guard if bi < 2 else (("v", deq_tok[bi - 2]),)
                    sk = f"w{par}"
                    E.dma("sync", wbuf[par][:].rearrange("p (c n) -> p c n", c=16),
                          w_h[:, hh * 16:(hh + 1) * 16, nc_ * 128:(nc_ + 1) * 128],
                          waits=wt, sem=sk)
                    E.dma("sync", mbuf[par][:].rearrange("p (c n) -> p c n", c=16),
                          m_h[:, hh * 16:(hh + 1) * 16, nc_ * 128:(nc_ + 1) * 128], sem=sk)
                    dw = [(sk, E.cnt[sk])]
                    if bi >= 2:
                        dw.append(("t", grp_tok[nc_ - 1]))
                    E.v("tensor_tensor", out=wd[par][:], in0=wbuf[par][:], in1=mbuf[par][:],
                        op=ALU.mult, waits=tuple(dw), incs=(("v", 1),))
                    deq_tok[bi] = E.cnt["v"]
                    for kc in range(16):
                        w_ = []
                        if kc == 0:
                            w_.append(("v", deq_tok[bi]))
                            if hh == 0:
                                w_.append(xtok)
                                if nc_ >= 2:
                                    w_.append(("v", evac_tok[nc_ - 2]))
                        E.t("matmul", out=ps_m[nc_ % 2][:],
                            lhsT=wd[par][:, kc * 128:(kc + 1) * 128],
                            rhs=xh[:, (hh * 16 + kc) * TH:(hh * 16 + kc + 1) * TH],
                            start=(hh == 0 and kc == 0), stop=(hh == 1 and kc == 15),
                            waits=tuple(w_), skip_group_check=True,
                            incs=(("t", 1),) if (hh == 1 and kc == 15) else ())
                    bi += 1
                mm_tok = E.cnt["t"]
                grp_tok[nc_] = mm_tok
                E.v("tensor_tensor", out=s_tile[:, nc_ * TH:(nc_ + 1) * TH],
                    in0=ps_m[nc_ % 2][:], in1=xh[:, nc_ * TH:(nc_ + 1) * TH],
                    op=ALU.add, waits=(("t", mm_tok),), incs=(("v", 1),))
                evac_tok[nc_] = E.cnt["v"]
            stok = ("v", E.cnt["v"])
            ptok = stats_ln(E, D, s_tile, 32, 1e-5, stok, sq, ps1, ps2,
                            pspan, mu_pan, r_pan)
            emit_ln_out(E, D, s_tile, x_dst, h, final, mu_pan, r_pan, ptok, fout, fout)

        for h in range(2):
            E.run_phase(phase, h)



DEBUG_L0F = 5


def emit_l0f(E, nc, D, x_src, x_dst, f32_src, final):
    assert not f32_src
    u0 = uid()
    with contextlib.ExitStack() as ph:
        sb = lambda n_, sh, dt: ph.enter_context(nc.sbuf_tensor(f"{n_}_{u0}", list(sh), dt))
        xh = sb("xh", (128, 32 * TH), BF16)
        at_bf = sb("at_bf", (128, 2048), BF16)
        bt_bf = sb("bt_bf", (16, 4 * 4096), BF16)
        wrt_bf = sb("wrt_bf", (128, 128), BF16)
        stage = sb("stage", (128, 4096), F32)
        lg = sb("lg", (128, 16), F32)
        gt = sb("gt", (128, 16), F32)
        gt2 = sb("gt2", (128, 8), F32)
        m1 = sb("m1", (128, 4), F32)
        m2 = sb("m2", (128, 4), F32)
        nm1 = sb("nm1", (128, 4), F32)
        kp = sb("kp", (128, 16), F32)
        ex = sb("ex", (128, 16), F32)
        en = sb("en", (128, 16), F32)
        den = sb("den", (128, 4), F32)
        rden = sb("rden", (128, 4), F32)
        gates = sb("gates", (128, 16), F32)
        midp = sb("midp", (128, 64), BF16)
        mid_im = sb("mid_im", (16, 4 * TH), BF16)
        sq = [sb(f"sq{i}", (128, TH), BF16) for i in range(2)]
        mu_pan = sb("mu_pan", (128, TH), BF16)
        r_pan = sb("r_pan", (128, TH), BF16)
        fout = [sb(f"fo{i}", (128, TH), F32 if final else BF16) for i in range(2)]
        ps_m = [ph.enter_context(nc.psum_tensor(f"psm{i}_{uid()}", [128, TH], F32)) for i in range(2)]
        ps_small = ph.enter_context(nc.psum_tensor(f"ps_small_{uid()}", [128, 16], F32))
        ps_mid_t = ph.enter_context(nc.psum_tensor(f"ps_mid_t_{uid()}", [128, 64], F32))
        ps_tr = ph.enter_context(nc.psum_tensor(f"ps_tr_{uid()}", [16, 128], BF16))
        ps1 = ph.enter_context(nc.psum_tensor(f"ps1_{uid()}", [1, TH], F32))
        ps2 = ph.enter_context(nc.psum_tensor(f"ps2_{uid()}", [1, TH], F32))
        pspan = ph.enter_context(nc.psum_tensor(f"pspan_{uid()}", [128, TH], F32))
        s_tile = D["A0"][:, 0:32 * TH]

        consts_tok = [None]

        def ph_consts(E):
            guard = E.guard()
            E.dma("sync", stage[:, 0:2048], D["l0f_at"], waits=guard, sem="r0")
            E.dma("sync", stage[:, 2048:2176], D["l0f_wrt"], sem="r0")
            rtok = E.cnt["r0"]
            E.v("tensor_scalar", out=at_bf[:], in0=stage[:, 0:2048], scalar1=1.0,
                scalar2=None, op0=ALU.mult, waits=(("r0", rtok),))
            E.v("tensor_scalar", out=wrt_bf[:], in0=stage[:, 2048:2176], scalar1=1.0,
                scalar2=None, op0=ALU.mult, incs=(("v", 1),))
            for e in range(4):
                E.dma("sync", stage[0:16, 0:4096], D["l0f_bt"][:, e * 4096:(e + 1) * 4096],
                      waits=(("v", E.cnt["v"]),), sem="r1")
                E.v("tensor_scalar", out=bt_bf[:, e * 4096:(e + 1) * 4096],
                    in0=stage[0:16, 0:4096], scalar1=1.0, scalar2=None, op0=ALU.mult,
                    waits=(("r1", E.cnt["r1"]),), incs=(("v", 1),))
            consts_tok[0] = ("v", E.cnt["v"])
        E.run_phase(ph_consts)

        def phase(E, h):
            guard = E.guard()
            xtok = load_xh(E, D, xh, x_src, h, False, None)
            ctok = E.ctok

            if DEBUG_L0F == 1:
                pass
            def ident_out():
                for kc in range(32):
                    E.v("tensor_scalar", out=s_tile[:, kc * TH:(kc + 1) * TH],
                        in0=xh[:, kc * TH:(kc + 1) * TH], scalar1=1.0, scalar2=None,
                        op0=ALU.mult, waits=(xtok,) if kc == 0 else (),
                        incs=(("v", 1),) if kc == 31 else ())
                stok_ = ("v", E.cnt["v"])
                ptok_ = stats_ln(E, D, s_tile, 32, 1e-5, stok_, sq, ps1, ps2,
                                 pspan, mu_pan, r_pan)
                emit_ln_out(E, D, s_tile, x_dst, h, final, mu_pan, r_pan, ptok_, fout, fout)
            if DEBUG_L0F == 1:
                return ident_out()
            for tc in range(4):
                for kc in range(32):
                    w_ = []
                    if kc == 0:
                        w_ = [xtok, consts_tok[0]] if tc == 0 else []
                    E.t("matmul", out=ps_small[:, tc * 4:(tc + 1) * 4],
                        lhsT=xh[:, (kc * TH) + tc * 128: (kc * TH) + (tc + 1) * 128],
                        rhs=wrt_bf[:, kc * 4:(kc + 1) * 4],
                        start=(kc == 0), stop=(kc == 31), waits=tuple(w_),
                        skip_group_check=True,
                        incs=(("t", 1),) if kc == 31 else ())
            lgt = E.cnt["t"]
            E.v("tensor_scalar", out=lg[:], in0=ps_small[:, 0:16], scalar1=1.0, scalar2=None,
                op0=ALU.mult, waits=(("t", lgt),), incs=(("v", 1),))
            if E.cur == "vector":
                E.eng.wait_ge(E.sems["v"], E.cnt["v"])
            if DEBUG_L0F == 2:
                return ident_out()
            def vsync():
                E.v("memset", D["vdummy"][0:1, 0:8], 0.0, incs=(("v", 1),))
                if E.cur == "vector":
                    E.eng.wait_ge(E.sems["v"], E.cnt["v"])
            C = lambda t_, i: lg[:, t_ * 4 + i: t_ * 4 + i + 1]
            for tc in range(4):
                E.v("tensor_max", out=gt[:, tc * 4 + 0:tc * 4 + 1], in0=C(tc, 0), in1=C(tc, 1))
                E.v("tensor_max", out=gt[:, tc * 4 + 1:tc * 4 + 2], in0=C(tc, 2), in1=C(tc, 3))
                E.v("tensor_tensor", out=gt[:, tc * 4 + 2:tc * 4 + 3], in0=C(tc, 0), in1=C(tc, 1), op=ALU.min)
                E.v("tensor_tensor", out=gt[:, tc * 4 + 3:tc * 4 + 4], in0=C(tc, 2), in1=C(tc, 3), op=ALU.min)
            vsync()
            G = lambda t_, i: gt[:, t_ * 4 + i: t_ * 4 + i + 1]
            for tc in range(4):
                E.v("tensor_max", out=m1[:, tc:tc + 1], in0=G(tc, 0), in1=G(tc, 1))
                E.v("tensor_tensor", out=gt2[:, tc * 2:tc * 2 + 1], in0=G(tc, 0), in1=G(tc, 1), op=ALU.min)
                E.v("tensor_max", out=gt2[:, tc * 2 + 1:tc * 2 + 2], in0=G(tc, 2), in1=G(tc, 3))
            vsync()
            for tc in range(4):
                E.v("tensor_max", out=m2[:, tc:tc + 1], in0=gt2[:, tc * 2:tc * 2 + 1],
                    in1=gt2[:, tc * 2 + 1:tc * 2 + 2])
            vsync()
            for tc in range(4):
                E.v("tensor_scalar", out=kp[:, tc * 4:(tc + 1) * 4], in0=lg[:, tc * 4:(tc + 1) * 4],
                    scalar1=m2[:, tc:tc + 1], scalar2=None, op0=ALU.is_ge)
                E.v("tensor_scalar", out=nm1[:, tc:tc + 1], in0=m1[:, tc:tc + 1],
                    scalar1=-1.0, scalar2=None, op0=ALU.mult,
                    incs=(("v", 1),) if tc == 3 else ())
            vtok = E.cnt["v"]
            for tc in range(4):
                E.s("activation", out=ex[:, tc * 4:(tc + 1) * 4], in_=lg[:, tc * 4:(tc + 1) * 4],
                    func=AF.Exp, bias=nm1[:, tc:tc + 1],
                    waits=(("v", vtok),) if tc == 0 else (),
                    incs=(("s", 1),) if tc == 3 else ())
            stok = E.cnt["s"]
            for tc in range(4):
                E.v("tensor_mul", out=en[:, tc * 4:(tc + 1) * 4], in0=ex[:, tc * 4:(tc + 1) * 4],
                    in1=kp[:, tc * 4:(tc + 1) * 4], waits=(("s", stok),) if tc == 0 else ())
            vsync()
            for tc in range(4):
                E.v("reduce_sum", out=den[:, tc:tc + 1], in_=en[:, tc * 4:(tc + 1) * 4],
                    axis=AX.X, incs=(("v", 1),) if tc == 3 else ())
            vtok = E.cnt["v"]
            E.s("activation", out=rden[:], in_=den[:], func=AF.Ln, waits=(("v", vtok),))
            E.s("activation", out=rden[:], in_=rden[:], func=AF.Exp, scale=-1.0, incs=(("s", 1),))
            stok = E.cnt["s"]
            for tc in range(4):
                E.v("tensor_scalar", out=gates[:, tc * 4:(tc + 1) * 4], in0=en[:, tc * 4:(tc + 1) * 4],
                    scalar1=rden[:, tc:tc + 1], scalar2=4.0, op0=ALU.mult, op1=ALU.mult,
                    waits=(("s", stok),) if tc == 0 else (),
                    incs=(("v", 1),) if tc == 3 else ())
            gtok = E.cnt["v"]
            if h == 0:
                E.dma("gpsimd", D["dbg"][:, 0:16], lg[:], waits=(("v", gtok),), sem="r0")
                E.dma("gpsimd", D["dbg"][:, 16:32], gt[:], sem="r0")
                E.dma("gpsimd", D["dbg"][:, 32:36], m1[:], sem="r0")
                E.dma("gpsimd", D["dbg"][:, 36:40], m2[:], sem="r0")
                E.dma("gpsimd", D["dbg"][:, 40:56], kp[:], sem="r0")
                E.dma("gpsimd", D["dbg"][:, 56:60], den[:], sem="r0")
                E.dma("gpsimd", D["dbg"][:, 60:64], rden[:], sem="r0")
                E.dma("gpsimd", D["dbg"][:, 64:80], en[:], sem="r0")
                E.dma("gpsimd", D["dbg"][:, 80:96], gates[:], sem="r0")
            if DEBUG_L0F == 3:
                return ident_out()
            for tc in range(4):
                for kc in range(32):
                    w_ = []
                    if kc == 0 and tc == 0:
                        w_ = [xtok]
                    if kc == 0 and tc > 0:
                        w_ = [("v", E.cnt["v"])]
                    E.t("matmul", out=ps_mid_t[:],
                        lhsT=xh[:, (kc * TH) + tc * 128: (kc * TH) + (tc + 1) * 128],
                        rhs=at_bf[:, kc * 64:(kc + 1) * 64],
                        start=(kc == 0), stop=(kc == 31), waits=tuple(w_),
                        skip_group_check=True,
                        incs=(("t", 1),) if kc == 31 else ())
                mtok = E.cnt["t"]
                if DEBUG_L0F == 47:
                    E.v("tensor_scalar", out=midp[:, 0:64], in0=ps_mid_t[:],
                        scalar1=1.0, scalar2=None, op0=ALU.mult,
                        waits=(("t", mtok),), incs=(("v", 1),))
                    continue
                for e in range(4):
                    E.v("tensor_scalar", out=midp[:, e * 16:(e + 1) * 16],
                        in0=ps_mid_t[:, e * 16:(e + 1) * 16],
                        scalar1=(1.0 if DEBUG_L0F == 46 else gates[:, tc * 4 + e:tc * 4 + e + 1]),
                        scalar2=None,
                        op0=ALU.mult,
                        waits=(("t", mtok), ("v", gtok)) if e == 0 else (),
                        incs=(("v", 1),) if e == 3 else ())
                ptok = E.cnt["v"]
                for e in range(4):
                    if DEBUG_L0F in (45, 46):
                        E.v("tensor_scalar",
                            out=mid_im[:, e * TH + tc * 128: e * TH + (tc + 1) * 128]
                            .rearrange("r (a b) -> r a b", a=8) if False else
                            mid_im[0:16, e * TH + tc * 128: e * TH + tc * 128 + 16],
                            in0=midp[0:16, e * 16:(e + 1) * 16],
                            scalar1=1.0, scalar2=None, op0=ALU.mult,
                            waits=(("v", ptok),) if e == 0 else (), incs=(("v", 1),))
                        continue
                    E.t("transpose", out=ps_tr[:], in_=midp[:, e * 16:(e + 1) * 16],
                        identity=D["ident"][:],
                        waits=(("v", ptok),) if e == 0 else (("v", E.cnt["v"]),),
                        incs=(("t", 1),))
                    trtok = E.cnt["t"]
                    E.v("tensor_scalar", out=mid_im[:, e * TH + tc * 128: e * TH + (tc + 1) * 128],
                        in0=ps_tr[0:16, 0:128], scalar1=1.0, scalar2=None, op0=ALU.mult,
                        waits=(("t", trtok),), incs=(("v", 1),))
            mimtok = ("v", E.cnt["v"])
            if DEBUG_L0F in (4, 45, 46, 47):
                return ident_out()
            evac_tok = {}
            for mc in range(32):
                for e in range(4):
                    w_ = []
                    if e == 0:
                        w_ = [mimtok]
                        if mc >= 2:
                            w_.append(("v", evac_tok[mc - 2]))
                    E.t("matmul", out=ps_m[mc % 2][:],
                        lhsT=bt_bf[:, e * 4096 + mc * 128: e * 4096 + (mc + 1) * 128],
                        rhs=mid_im[:, e * TH:(e + 1) * TH],
                        start=(e == 0), stop=(e == 3), waits=tuple(w_),
                        skip_group_check=True,
                        incs=(("t", 1),) if e == 3 else ())
                mmt = E.cnt["t"]
                E.v("tensor_tensor", out=s_tile[:, mc * TH:(mc + 1) * TH],
                    in0=ps_m[mc % 2][:], in1=xh[:, mc * TH:(mc + 1) * TH],
                    op=ALU.add, waits=(("t", mmt),), incs=(("v", 1),))
                evac_tok[mc] = E.cnt["v"]
            stok2 = ("v", E.cnt["v"])
            ptok2 = stats_ln(E, D, s_tile, 32, 1e-5, stok2, sq, ps1, ps2,
                             pspan, mu_pan, r_pan)
            emit_ln_out(E, D, s_tile, x_dst, h, final, mu_pan, r_pan, ptok2, fout, fout)

        for h in range(2):
            E.run_phase(phase, h)


_UID = [0]


def uid():
    _UID[0] += 1
    return _UID[0]


def ap3(ap, off, s1, n1, s2, n2):
    return bass.AP(ap.tensor, ap.offset + off, [ap.ap[0], [s1, n1], [s2, n2]])


def vwait(E):
    if E.cur == "vector":
        E.eng.wait_ge(E.sems["v"], E.cnt["v"])


def emit_proj_packed(E, D, T, PS, n_oc, words_ap, s_ap, z_ap, rhs_fn, sink_fn,
                     extra_mm_wait=None):
    w_int, nib, w_deq = T["w_int"], T["nib"], T["w_deq"]
    pan_z, pan_s = T["pan_z"], T["pan_s"]
    s_bf, z_w, z_int, zbf = T["s_bf"], T["z_w"], T["z_int"], T["zbf"]
    ps_pan = PS["ps_pan"]
    ps_m = [PS["psm0"], PS["psm1"]]
    n_sup = 4
    extr_tok = {}
    pan_tok = {}
    sink_tok = {}
    mmh = {}
    cv_tok = {}
    dq_tok = {}
    for sp in range((n_oc + n_sup - 1) // n_sup):
        ocs = list(range(sp * n_sup, min((sp + 1) * n_sup, n_oc)))
        nc_ = len(ocs)
        ncw = nc_ * 16
        ncc = nc_ * 128
        spar = sp % 2
        sk = f"w{spar}"
        wt = (("v", extr_tok[sp - 2]),) if sp >= 2 else T["guard"]
        E.dma("sync", w_int[spar][:, 0:32 * ncw].rearrange("p (c w) -> p c w", c=32),
              words_ap[:, :, ocs[0] * 16: ocs[0] * 16 + ncw], waits=wt, sem=sk)
        wtok = (sk, E.cnt[sk])
        rk = f"r{spar}"
        gk = "ga" if spar == 0 else "gb"
        rt = (("v", pan_tok[sp - 2]),) if sp >= 2 else T["guard"]
        for hh2 in range(2):
            E.dma("gpsimd", s_bf[spar][hh2][0:16, 0:ncc],
                  s_ap[16 * hh2:16 * hh2 + 16, ocs[0] * 128: ocs[0] * 128 + ncc],
                  waits=rt if hh2 == 0 else (), sem=gk)
            E.dma("sync", z_w[spar][hh2][0:16, 0:ncw],
                  z_ap[16 * hh2:16 * hh2 + 16, ocs[0] * 16: ocs[0] * 16 + ncw], sem=rk)
        rtok = (rk, E.cnt[rk])
        gtok = (gk, E.cnt[gk])
        for hh2 in range(2):
            zw = z_w[spar][hh2]
            for j2 in range(4):
                E.v("tensor_scalar",
                    out=ap3(z_int[0:16, :], 16 * j2, 64, nc_, 1, 16),
                    in0=ap3(zw[0:16, :], 0, 16, nc_, 1, 16),
                    scalar1=4 * j2, scalar2=0x000F000F,
                    op0=ALU.logical_shift_right, op1=ALU.bitwise_and,
                    waits=(rtok,) if (j2 == 0 and hh2 == 0) else (),
                    incs=(("v", 1),) if j2 == 3 else ())
            vwait(E)
            E.v("tensor_scalar", out=zbf[0:16, 0:ncc],
                in0=z_int[0:16, :].bitcast(I16)[0:16, 0:ncc],
                scalar1=1.0, scalar2=None, op0=ALU.mult, incs=(("v", 1),))
            E.t("matmul", out=ps_pan[:, 0:ncc], lhsT=D["r16"][:],
                rhs=zbf[0:16, 0:ncc], start=True, stop=True,
                waits=(("v", E.cnt["v"]), ("c", 64)), skip_group_check=True,
                incs=(("t", 1),))
            E.v("tensor_scalar", out=pan_z[hh2][:, 0:ncc], in0=ps_pan[:, 0:ncc],
                scalar1=1.0, scalar2=None, op0=ALU.mult,
                waits=(("t", E.cnt["t"]),), incs=(("v", 1),))
            E.t("matmul", out=ps_pan[:, 0:ncc], lhsT=D["r16"][:],
                rhs=s_bf[spar][hh2][0:16, 0:ncc], start=True, stop=True,
                waits=(("v", E.cnt["v"]), gtok), skip_group_check=True,
                incs=(("t", 1),))
            E.v("tensor_scalar", out=pan_s[hh2][:, 0:ncc], in0=ps_pan[:, 0:ncc],
                scalar1=1.0, scalar2=None, op0=ALU.mult,
                waits=(("t", E.cnt["t"]),), incs=(("v", 1),))
            vwait(E)
        pan_tok[sp] = E.cnt["v"]
        for oi, oc in enumerate(ocs):
            for hh2 in range(2):
                hid = oc * 2 + hh2
                dpar = hid % 2
                for j2 in range(4):
                    w_ = [wtok] if (j2 == 0 and oi == 0 and hh2 == 0) else []
                    E.v("tensor_scalar",
                        out=ap3(nib[dpar][:], 16 * j2, 64, 16, 1, 16),
                        in0=ap3(w_int[spar][:], hh2 * 16 * ncw + oi * 16, ncw, 16, 1, 16),
                        scalar1=4 * j2, scalar2=0x000F000F,
                        op0=ALU.logical_shift_right, op1=ALU.bitwise_and,
                        waits=tuple(w_), incs=(("v", 1),) if j2 == 3 else ())
                vwait(E)
                dq_w = []
                if hid >= 2:
                    dq_w.append(("t", mmh[hid - 2]))
                E.v("tensor_tensor", out=ap3(w_deq[dpar][:], 0, 128, 16, 1, 128),
                    in0=ap3(nib[dpar][:].bitcast(I16), 0, 128, 16, 1, 128),
                    in1=ap3(pan_z[hh2][:], oi * 128, 0, 16, 1, 128),
                    op=ALU.subtract, waits=tuple(dq_w), incs=(("v", 1),))
                dq_tok[hid] = E.cnt["v"]
                E.v("tensor_tensor", out=ap3(w_deq[dpar][:], 0, 128, 16, 1, 128),
                    in0=ap3(w_deq[dpar][:], 0, 128, 16, 1, 128),
                    in1=ap3(pan_s[hh2][:], oi * 128, 0, 16, 1, 128),
                    op=ALU.mult, incs=(("v", 1),))
                dq = E.cnt["v"]
                for k2 in range(16):
                    kc = hh2 * 16 + k2
                    w_ = []
                    if k2 == 0:
                        w_.append(("v", dq))
                        if hh2 == 0:
                            war = sink_tok.get(oc - 2) or sink_tok.get(oc - 1)
                            if oc >= 1 and war:
                                w_.append(war)
                            if oc == 0 and extra_mm_wait is not None:
                                w_.append(extra_mm_wait)
                    E.t("matmul", out=ps_m[oc % 2][:],
                        lhsT=w_deq[dpar][:, k2 * 128:(k2 + 1) * 128],
                        rhs=rhs_fn(kc), start=(kc == 0), stop=(kc == 31),
                        waits=tuple(w_), skip_group_check=True, incs=(("t", 1),))
                mmh[hid] = E.cnt["t"]
            T["mm_done"] = mmh[oc * 2 + 1]
            sink_tok[oc] = sink_fn(E, oc, ps_m[oc % 2])
        extr_tok[sp] = E.cnt["v"]


DEBUG_L1A = None


def dump_dbg2(E, D, tile, off=0):
    d2 = D["dbg2"].rearrange("(c p) t -> p c t", p=128)
    E.dma("gpsimd", d2,
          tile[:, off:off + 32 * TH].rearrange("p (c t) -> p c t", c=32),
          waits=(("v", E.cnt["v"]), ("t", E.cnt["t"]), ("s", E.cnt["s"])), sem="o0")


def emit_l1a(E, nc, D, x_src, x_dst, f32_src, final):
    assert not f32_src
    x_h = D[x_src].rearrange("(c p) t -> p c t", p=128)
    xr_h = D["xr_buf"].rearrange("(c p) t -> p c t", p=128)
    qkv_w = D["qkv_qw"].rearrange("(c p) w -> p c w", p=128)
    o_w = D["o_qw"].rearrange("(c p) w -> p c w", p=128)
    gu_w = D["gu_qw"].rearrange("(c p) w -> p c w", p=128)
    A0 = D["A0"]

    u0 = uid()
    with contextlib.ExitStack() as ph:
        sb = lambda n_, sh, dt: ph.enter_context(nc.sbuf_tensor(f"{n_}_{u0}", list(sh), dt))
        hh = sb("hh", (128, 32 * TH), BF16)
        lnw = [sb("lnw1", (128, 32), F32), sb("lnw2", (128, 32), F32)]
        sq = [sb(f"sq{i}", (128, TH), BF16) for i in range(2)]
        r_pan = sb("r_pan", (128, TH), BF16)
        mu_pan = sb("mu_pan", (128, TH), BF16)
        fout = [sb(f"fo{i}", (128, TH), F32 if final else BF16) for i in range(2)]
        T = dict()
        TT = dict()
        PS = {}

        lnw_tok = [None]

        def ph_lnw(E):
            E.dma("sync", lnw[0][:], D["ln1_w"], waits=E.guard(), sem="r0")
            E.dma("sync", lnw[1][:], D["ln2_w"], sem="r0")
            lnw_tok[0] = ("r0", E.cnt["r0"])
        E.run_phase(ph_lnw)

        def rms_and_norm(E, h, src_tile, dst_tile, lnw_t, from_hbm):
            guard = T["guard"]
            mt, xt2 = {}, {}
            for kc in range(32):
                par = kc % 2
                if from_hbm:
                    w_ = (("s", xt2[kc - 2]),) if kc >= 2 else guard
                    E.dma("scalar", TT["xc"][par][:], x_h[:, kc, h * TH:(h + 1) * TH],
                          waits=w_, sem=f"x{par}")
                    srcap = TT["xc"][par][:]
                    sqw = [(f"x{par}", E.cnt[f"x{par}"])]
                else:
                    srcap = src_tile[:, kc * TH:(kc + 1) * TH]
                    sqw = list(guard) if kc == 0 else []
                if kc >= 2:
                    sqw.append(("t", mt[kc - 2]))
                E.s("activation", out=sq[par][:], in_=srcap, func=AF.Square,
                    waits=tuple(sqw), incs=(("s", 1),))
                xt2[kc] = E.cnt["s"]
                E.t("matmul", out=PS["ps2"][0:1, :], lhsT=D["ones128"][:], rhs=sq[par][:],
                    start=(kc == 0), stop=(kc == 31),
                    waits=(("s", E.cnt["s"]), ("c", 64)), skip_group_check=True,
                    incs=(("t", 1),))
                mt[kc] = E.cnt["t"]
            ttok = E.cnt["t"]
            rows = D["rows"]
            R = lambda i: rows[0:1, i * TH:(i + 1) * TH]
            E.v("tensor_scalar", out=R(0), in0=PS["ps2"][0:1, :], scalar1=1.0 / 4096,
                scalar2=None, op0=ALU.mult, waits=(("t", ttok),), incs=(("v", 1),))
            vwait(E)
            E.s("activation", out=R(0), in_=R(0), func=AF.Ln,
                bias=D["ceps"][0:1, 1:2], waits=(("v", E.cnt["v"]),))
            E.s("activation", out=R(0), in_=R(0), func=AF.Exp, scale=-0.5,
                incs=(("s", 1),))
            E.v("tensor_scalar", out=D["rows_bf"][0:1, 0:TH], in0=R(0), scalar1=1.0,
                scalar2=None, op0=ALU.mult, waits=(("s", E.cnt["s"]),), incs=(("v", 1),))
            E.t("matmul", out=PS["pspan"][:], lhsT=D["ones1"][:],
                rhs=D["rows_bf"][0:1, 0:TH], start=True, stop=True,
                waits=(("v", E.cnt["v"]),), skip_group_check=True, incs=(("t", 1),))
            E.v("tensor_scalar", out=r_pan[:], in0=PS["pspan"][:], scalar1=1.0,
                scalar2=None, op0=ALU.mult, waits=(("t", E.cnt["t"]),), incs=(("v", 1),))
            vwait(E)
            xt = {}
            for kc in range(32):
                par = kc % 2
                mw = []
                if from_hbm:
                    w_ = (("v", xt[kc - 2]),) if kc >= 2 else ()
                    E.dma("scalar", TT["xc"][par][:], x_h[:, kc, h * TH:(h + 1) * TH],
                          waits=w_, sem=f"x{par}")
                    srcap = TT["xc"][par][:]
                    mw = [(f"x{par}", E.cnt[f"x{par}"])]
                else:
                    srcap = src_tile[:, kc * TH:(kc + 1) * TH]
                if kc == 0:
                    mw.append(lnw_tok[0])
                E.v("tensor_mul", out=dst_tile[:, kc * TH:(kc + 1) * TH],
                    in0=srcap, in1=r_pan[:], waits=tuple(mw), incs=(("v", 1),))
                vwait(E)
                E.v("tensor_scalar", out=dst_tile[:, kc * TH:(kc + 1) * TH],
                    in0=dst_tile[:, kc * TH:(kc + 1) * TH],
                    scalar1=lnw_t[:, kc:kc + 1], scalar2=None, op0=ALU.mult,
                    incs=(("v", 1),))
                xt[kc] = E.cnt["v"]

        def phA(E, h):
            T["guard"] = E.guard()
            rms_and_norm(E, h, None, hh, lnw[0], True)
            h1_tok = ("v", E.cnt["v"])
            if h == 0 and DEBUG_L1A == "h1":
                dump_dbg2(E, D, hh)

            def sink_q(E, oc, ps):
                E.v("tensor_scalar", out=A0[:, oc * TH:(oc + 1) * TH], in0=ps[:],
                    scalar1=1.0, scalar2=None, op0=ALU.mult,
                    waits=(("t", T["mm_done"]),), incs=(("v", 1),))
                return ("v", E.cnt["v"])
            emit_proj_packed(E, D, T, PS, 32, qkv_w, D["qkv_sc"], D["qkv_qz"],
                             lambda kc: hh[:, kc * TH:(kc + 1) * TH], sink_q,
                             extra_mm_wait=h1_tok)
            if h == 0 and DEBUG_L1A == "q":
                dump_dbg2(E, D, A0)

        def phB(E, h):
            T["guard"] = E.guard()
            guard = T["guard"]
            xro = {}

            def sink_o(E, mc, ps):
                par = mc % 2
                w_ = ((f"o{par}", xro[mc - 2]),) if mc >= 2 else guard
                E.dma("scalar", TT["xc"][par][:], x_h[:, mc, h * TH:(h + 1) * TH],
                      waits=w_, sem=f"x{par}")
                E.v("tensor_tensor", out=hh[:, mc * TH:(mc + 1) * TH], in0=ps[:],
                    in1=TT["xc"][par][:], op=ALU.add,
                    waits=(("t", T["mm_done"]), (f"x{par}", E.cnt[f"x{par}"])),
                    incs=(("v", 1),))
                vwait(E)
                E.v("tensor_tensor", out=TT["xc"][par][:], in0=TT["xc"][par][:],
                    in1=hh[:, mc * TH:(mc + 1) * TH], op=ALU.add, incs=(("v", 1),))
                tok = ("v", E.cnt["v"])
                E.dma("gpsimd", xr_h[:, mc, :], TT["xc"][par][:],
                      waits=(("v", E.cnt["v"]),), sem=f"o{par}")
                xro[mc] = E.cnt[f"o{par}"]
                return tok
            emit_proj_packed(E, D, T, PS, 32, o_w, D["o_sc"], D["o_qz"],
                             lambda kc: A0[:, kc * TH:(kc + 1) * TH], sink_o)
            if h == 0 and DEBUG_L1A == "r2":
                dump_dbg2(E, D, hh)

        def phC(E, h):
            T["guard"] = E.guard()
            if h == 0 and DEBUG_L1A == "xr":
                E.dma("gpsimd", D["dbg2"], D["xr_buf"], waits=T["guard"], sem="o0")
            rms_and_norm(E, h, hh, hh, lnw[1], False)
            if h == 0 and DEBUG_L1A == "h2":
                dump_dbg2(E, D, hh)

        def phD(E, h):
            T["guard"] = E.guard()

            def sink_gu(E, oc, ps):
                ci = oc // 2
                if oc % 2 == 0:
                    return None
                E.s("activation", out=TT["silu_t"][:], in_=PS["psm0"][:], func=AF.Silu,
                    waits=(("t", T["mm_done"]),), incs=(("s", 1),))
                E.v("tensor_tensor", out=A0[:, ci * TH:(ci + 1) * TH], in0=ps[:],
                    in1=TT["silu_t"][:], op=ALU.mult,
                    waits=(("s", E.cnt["s"]),), incs=(("v", 1),))
                return ("v", E.cnt["v"])
            emit_proj_packed(E, D, T, PS, 172, gu_w, D["gu_sc"], D["gu_qz"],
                             lambda kc: hh[:, kc * TH:(kc + 1) * TH], sink_gu)
            if h == 0 and DEBUG_L1A == "a0":
                dump_dbg2(E, D, A0)
            if h == 0 and DEBUG_L1A == "a40":
                dump_dbg2(E, D, A0, off=40 * TH)

        def phE(E, h):
            T["guard"] = E.guard()
            emit_dn(E, nc, D, T, h, x_dst, final, PS,
                    dn_w=D["dn_qw"].rearrange("(c p) w -> p c w", p=128),
                    dn_s=D["dn_sc"], dn_z=D["dn_qz"],
                    res_h=xr_h, res_half=False,
                    s_tile=hh, sq=sq, r_pan=r_pan, mu_pan=mu_pan, fout=fout,
                    a_tile=A0, packed=True)

        for h in range(2):
            with contextlib.ExitStack() as pp:
                u2 = uid()
                sb2 = lambda n_, sh, dt: pp.enter_context(
                    nc.sbuf_tensor(f"{n_}_{u2}", list(sh), dt))
                TT.clear()
                TT["xc"] = [sb2(f"xct{i}", (128, TH), BF16) for i in range(2)]
                TT["w_int"] = [sb2(f"wi{i}", (128, 2048), I32) for i in range(2)]
                TT["nib"] = [sb2(f"nib{i}", (128, 1024), I32) for i in range(2)]
                TT["w_deq"] = [sb2(f"wd{i}", (128, 2048), BF16) for i in range(2)]
                TT["s_bf"] = [[sb2(f"sbf{i}{j}", (16, 512), BF16) for j in range(2)]
                              for i in range(2)]
                TT["z_w"] = [[sb2(f"zwt{i}{j}", (16, 64), I32) for j in range(2)]
                             for i in range(2)]
                TT["z_int"] = sb2("z_intt", (16, 256), I32)
                TT["zbf"] = sb2("zbft", (16, 512), BF16)
                TT["pan_z"] = [sb2(f"pzt{i}", (128, 512), BF16) for i in range(2)]
                TT["pan_s"] = [sb2(f"pnst{i}", (128, 512), BF16) for i in range(2)]
                TT["silu_t"] = sb2("silu_tt", (128, TH), BF16)
                T.update(TT)
                PS.clear()
                PS["psm0"] = pp.enter_context(nc.psum_tensor(f"psm0_{uid()}", [128, TH], F32))
                PS["psm1"] = pp.enter_context(nc.psum_tensor(f"psm1_{uid()}", [128, TH], F32))
                PS["ps_pan"] = pp.enter_context(nc.psum_tensor(f"ps_pan_{uid()}", [128, TH], F32))
                PS["ps2"] = pp.enter_context(nc.psum_tensor(f"ps2_{uid()}", [1, TH], F32))
                PS["pspan"] = pp.enter_context(nc.psum_tensor(f"pspan_{uid()}", [128, TH], F32))
                E.run_phase(phA, h)
                E.run_phase(phB, h)
                E.run_phase(phC, h)
                E.run_phase(phD, h)
            with contextlib.ExitStack() as pp:
                PS.clear()
                for i in range(8):
                    PS[f"ps8_{i}"] = pp.enter_context(
                        nc.psum_tensor(f"ps8x{i}_{uid()}", [128, TH], F32))
                T["dnT"] = alloc_dn_tiles(nc, pp, True)
                E.run_phase(phE, h)


DN_SUPERS = [(0, 7), (7, 7), (14, 7), (21, 7), (28, 4)]


def alloc_dn_tiles(nc, pp, packed):
    u = uid()
    sb = lambda n_, sh, dt: pp.enter_context(nc.sbuf_tensor(f"{n_}_{u}", list(sh), dt))
    dnT = {}
    if packed:
        dnT["dw"] = [sb(f"dnw{i}", (128, 128), I32) for i in range(2)]
        dnT["dnib"] = [sb(f"dnib{i}", (128, 512), I32) for i in range(2)]
        dnT["zrow"] = sb("zrow", (16, 128), I32)
        dnT["zint"] = sb("zint", (16, 512), I32)
    else:
        dnT["dw"] = [sb(f"dnw{i}", (128, 1024), I32) for i in range(2)]
        dnT["dnib"] = None
        dnT["zrow"] = sb("zrow", (16, 1024), F32)
        dnT["zint"] = None
    dnT["ddeq"] = [sb(f"dnd{i}", (128, 1024), BF16) for i in range(2)]
    dnT["srow"] = sb("srow", (16, 1024), F32)
    dnT["szbf"] = sb("szbf", (16, 2048), BF16)
    dnT["dpz"] = [sb(f"dpz{b}", (128, 1024), BF16) for b in range(6)]
    dnT["dps"] = [sb(f"dpsn{b}", (128, 1024), BF16) for b in range(6)]
    dnT["xr_t"] = [sb(f"xrt{i}", (128, TH), BF16) for i in range(2)]
    return dnT


def emit_dn(E, nc, D, T, h, x_dst, final, PS, dn_w, dn_s, dn_z, res_h, res_half,
            s_tile, sq, r_pan, mu_pan, fout, a_tile, packed, CI=86):
    ps8 = [PS[f"ps8_{i}"] for i in range(8)]
    dnT = T["dnT"]
    dw, dnib, ddeq = dnT["dw"], dnT["dnib"], dnT["ddeq"]
    srow, zrow, zint, szbf = dnT["srow"], dnT["zrow"], dnT["zint"], dnT["szbf"]
    dpz, dps, xr_t = dnT["dpz"], dnT["dps"], dnT["xr_t"]
    if True:
        guard = T["guard"]
        t0 = E.cnt["t"]
        sink_v = {}
        extr_of = {}

        def build_panel(E, rows0, nrows, lhsT, cols, slot, w_first, mw=1024):
            rk = f"r{slot % 2}"
            E.dma("sync", srow[0:nrows, 0:mw], dn_s[rows0:rows0 + nrows, cols],
                  waits=w_first, sem=rk)
            if packed:
                E.dma("sync", zrow[0:nrows, 0:mw // 8],
                      dn_z[rows0:rows0 + nrows, cols.start // 8:cols.start // 8 + mw // 8],
                      sem=rk)
            else:
                E.dma("sync", zrow[0:nrows, 0:mw], dn_z[rows0:rows0 + nrows, cols],
                      sem=rk)
            rtok = (rk, E.cnt[rk])
            if packed:
                for j2 in range(4):
                    E.v("tensor_scalar",
                        out=ap3(zint[0:nrows, :], 16 * j2, 64, mw // 128, 1, 16),
                        in0=ap3(zrow[0:nrows, :], 0, 16, mw // 128, 1, 16),
                        scalar1=4 * j2, scalar2=0x000F000F,
                        op0=ALU.logical_shift_right, op1=ALU.bitwise_and,
                        waits=(rtok,) if j2 == 0 else (),
                        incs=(("v", 1),) if j2 == 3 else ())
                vwait(E)
                E.v("tensor_scalar", out=szbf[0:nrows, 0:mw],
                    in0=zint[0:nrows, :].bitcast(I16)[0:nrows, 0:mw],
                    scalar1=1.0, scalar2=None, op0=ALU.mult, incs=(("v", 1),))
            else:
                E.v("tensor_scalar", out=szbf[0:nrows, 0:mw], in0=zrow[0:nrows, 0:mw],
                    scalar1=1.0, scalar2=None, op0=ALU.mult,
                    waits=(rtok,), incs=(("v", 1),))
            E.v("tensor_scalar", out=szbf[0:nrows, 1024:1024 + mw],
                in0=srow[0:nrows, 0:mw],
                scalar1=1.0, scalar2=None, op0=ALU.mult, incs=(("v", 1),))
            for b0 in range(0, mw, 512):
                bn = min(512, mw - b0)
                E.t("matmul", out=ps8[7][:, 0:bn], lhsT=lhsT,
                    rhs=szbf[0:nrows, b0:b0 + bn], start=True, stop=True,
                    waits=(("v", E.cnt["v"]), ("c", 64)), skip_group_check=True,
                    incs=(("t", 1),))
                E.v("tensor_scalar", out=dpz[slot][:, b0:b0 + bn], in0=ps8[7][:, 0:bn],
                    scalar1=1.0, scalar2=None, op0=ALU.mult,
                    waits=(("t", E.cnt["t"]),), incs=(("v", 1),))
                E.t("matmul", out=ps8[7][:, 0:bn], lhsT=lhsT,
                    rhs=szbf[0:nrows, 1024 + b0:1024 + b0 + bn], start=True, stop=True,
                    waits=(("v", E.cnt["v"]),), skip_group_check=True, incs=(("t", 1),))
                E.v("tensor_scalar", out=dps[slot][:, b0:b0 + bn], in0=ps8[7][:, 0:bn],
                    scalar1=1.0, scalar2=None, op0=ALU.mult,
                    waits=(("t", E.cnt["t"]),), incs=(("v", 1),))
            vwait(E)
            return ("v", E.cnt["v"])

        for si, (mc0, nmc) in enumerate(DN_SUPERS):
            mw = nmc * 128
            cols = slice(mc0 * 128, mc0 * 128 + mw)
            deq_done = {}
            mm_of = {}
            extr_of.clear()
            ptoks = {}
            scr_free = (("v", sink_v[si - 1]),) if si >= 1 else guard
            nat_slot = {ci: 5 for ci in range(80, CI)}
            for blk in range(5):
                w_first = scr_free if blk == 0 else (("v", ptoks[blk - 1][1]),)
                ptoks[blk] = build_panel(E, blk * 16, 16, D["r16"][:], cols, blk,
                                         w_first, mw)
            pan_all = ("v", E.cnt["v"])
            for ci in range(0, 80):
                blk = ci // 16
                emit_dn_chunk(E, D, dw, dnib, ddeq, dpz[blk], dps[blk], dn_w, ci,
                              cols, a_tile, ps8, deq_done, mm_of, extr_of, guard,
                              packed, CI, mw,
                              extra=([pan_all] if ci == 0 else [])
                              + ([("v", sink_v[si - 1]), ("t", t0)] if (ci == 0 and si >= 1) else [])
                              + ([("t", t0)] if (ci == 0 and si == 0) else []))
                if h == 0 and si == 0 and ci == 0 and DEBUG_L1A == "ddeq":
                    E.dma("gpsimd", D["dbg2"].rearrange("(a b) t -> a (b t)", b=2)[0:128, 0:mw],
                          ddeq[0][:, 0:mw], waits=(("v", deq_done[0]),), sem="o0")
            for ci in range(80, CI):
                slot = nat_slot[ci]
                w_first = (("v", deq_done[ci - 1]),) if ci >= 81 else (("v", E.cnt["v"]),)
                ptk = build_panel(E, ci, 1, D["ones1"][:], cols, slot, w_first, mw)
                emit_dn_chunk(E, D, dw, dnib, ddeq, dpz[slot], dps[slot], dn_w, ci,
                              cols, a_tile, ps8, deq_done, mm_of, extr_of, guard,
                              packed, CI, mw, extra=[ptk])
            mmt = E.cnt["t"]
            tt_tok = {}
            for m2 in range(nmc):
                mc = mc0 + m2
                par = mc % 2
                rsrc = res_h[:, mc, h * TH:(h + 1) * TH] if res_half else res_h[:, mc, :]
                if m2 >= 2:
                    w_ = (("v", tt_tok[m2 - 2]),)
                elif si >= 1:
                    w_ = (("v", sink_v[si - 1]),)
                else:
                    w_ = ()
                E.dma("scalar", xr_t[par][:], rsrc, waits=w_, sem=f"x{par}")
                E.v("tensor_tensor", out=s_tile[:, mc * TH:(mc + 1) * TH],
                    in0=ps8[m2][:], in1=xr_t[par][:], op=ALU.add,
                    waits=(("t", mmt), (f"x{par}", E.cnt[f"x{par}"])),
                    incs=(("v", 1),))
                tt_tok[m2] = E.cnt["v"]
            sink_v[si] = E.cnt["v"]
        if h == 0 and DEBUG_L1A == "s_pre":
            dump_dbg2(E, D, s_tile)
        stok = ("v", E.cnt["v"])
        ptok = stats_ln(E, D, s_tile, 32, 1e-5, stok, sq,
                        ps8[0][0:1, :], ps8[1][0:1, :], ps8[2][:],
                        mu_pan, r_pan)
        emit_ln_out(E, D, s_tile, x_dst, h, final, mu_pan, r_pan, ptok, fout, fout)


def emit_dn_chunk(E, D, dw, dnib, ddeq, pz, ps_, dn_w, ci, cols, a_tile, ps8,
                  deq_done, mm_of, extr_of, guard, packed, CI, mw, extra=()):
    par = ci % 2
    sk = f"w{par}"
    if packed:
        w_ = (("v", extr_of[ci - 2]),) if extr_of.get(ci - 2) else guard
        E.dma("sync", dw[par][:, 0:mw // 8],
              dn_w[:, ci, cols.start // 8: cols.start // 8 + mw // 8],
              waits=w_, sem=sk)
    else:
        w_ = (("v", deq_done[ci - 2]),) if deq_done.get(ci - 2) else guard
        E.dma("sync", dw[par][:, 0:mw], dn_w[:, ci, cols], waits=w_, sem=sk)
    wtok = (sk, E.cnt[sk])
    dpr = ci % 2
    if packed:
        for j2 in range(4):
            w_ = [wtok] if j2 == 0 else []
            E.v("tensor_scalar",
                out=ap3(dnib[par][:], 16 * j2, 64, mw // 128, 1, 16),
                in0=ap3(dw[par][:], 0, 16, mw // 128, 1, 16),
                scalar1=4 * j2, scalar2=0x000F000F,
                op0=ALU.logical_shift_right, op1=ALU.bitwise_and,
                waits=tuple(w_), incs=(("v", 1),) if j2 == 3 else ())
        vwait(E)
        src_nib = dnib[par][:].bitcast(I16)
        extr_of[ci] = E.cnt["v"]
        mmw0 = [("v", E.cnt["v"])]
    else:
        src_nib = dw[par]
        mmw0 = [wtok]
    mmw0 += list(extra)
    if ci >= 2 and mm_of.get(ci - 2):
        mmw0.append(("t", mm_of[ci - 2]))
    E.v("tensor_tensor", out=ddeq[dpr][:, 0:mw],
        in0=(src_nib[:, 0:mw] if not packed else
             bass.AP(src_nib.tensor, src_nib.offset, [src_nib.ap[0], [1, mw]])),
        in1=pz[:, 0:mw], op=ALU.subtract, waits=tuple(mmw0), incs=(("v", 1),))
    vwait(E)
    E.v("tensor_tensor", out=ddeq[dpr][:, 0:mw], in0=ddeq[dpr][:, 0:mw],
        in1=ps_[:, 0:mw], op=ALU.mult, incs=(("v", 1),))
    dq = E.cnt["v"]
    deq_done[ci] = dq
    for m2 in range(mw // 128):
        w_ = [("v", dq)] if m2 == 0 else []
        E.t("matmul", out=ps8[m2][:],
            lhsT=ddeq[dpr][:, m2 * 128:(m2 + 1) * 128],
            rhs=a_tile[:, ci * TH:(ci + 1) * TH],
            start=(ci == 0), stop=(ci == CI - 1),
            waits=tuple(w_), skip_group_check=True, incs=(("t", 1),))
    mm_of[ci] = E.cnt["t"]


def emit_l1f(E, nc, D, x_src, x_dst, f32_src, final):
    assert not f32_src
    x_h = D[x_src].rearrange("(c p) t -> p c t", p=128)
    qg = D["f_qg"].rearrange("(c p) i -> p c i", p=128)
    qu = D["f_qu"].rearrange("(c p) i -> p c i", p=128)
    A0 = D["A0"]

    u0 = uid()
    with contextlib.ExitStack() as ph:
        sb = lambda n_, sh, dt: ph.enter_context(nc.sbuf_tensor(f"{n_}_{u0}", list(sh), dt))
        xh = sb("xhf", (128, 32 * TH), BF16)
        sq = [sb(f"sqf{i}", (128, TH), BF16) for i in range(2)]
        r_pan = sb("r_panf", (128, TH), BF16)
        mu_pan = sb("mu_panf", (128, TH), BF16)
        fout = [sb(f"fof{i}", (128, TH), F32 if final else BF16) for i in range(2)]
        T = dict()
        TT = dict()
        PS = {}

        def phF1(E, h):
            T["guard"] = E.guard()
            guard = T["guard"]
            xtok = load_xh(E, D, xh, x_src, h, False, None)
            deq_tok = {}
            mm_tok = {}
            sink_tok = {}
            pan_tok = {}
            for ci in range(86):
                for mi, (qw_ap, s_ap, z_ap) in enumerate(
                        ((qg, D["f_sg"], D["f_zg"]), (qu, D["f_su"], D["f_zu"]))):
                    hid = 2 * ci + mi
                    par = hid % 2
                    sk = f"w{par}"
                    wt = (("v", deq_tok[hid - 2]),) if hid >= 2 else guard
                    E.dma("sync", TT["w_int"][par][:, 0:2048].rearrange("p (c i) -> p c i", c=16),
                          qw_ap[:, 0:16, ci * 128:(ci + 1) * 128], waits=wt, sem=sk)
                    E.dma("sync", TT["w_int"][par][:, 2048:4096].rearrange("p (c i) -> p c i", c=16),
                          qw_ap[:, 16:32, ci * 128:(ci + 1) * 128], sem=sk)
                    wtok = (sk, E.cnt[sk])
                    gk = "ga" if par == 0 else "gb"
                    rt = (("v", pan_tok[hid - 1]),) if hid >= 1 else guard
                    for hh2 in range(2):
                        E.dma("gpsimd", TT["s_bf"][hh2][0:16, 0:128],
                              s_ap[16 * hh2:16 * hh2 + 16, ci * 128:(ci + 1) * 128],
                              waits=rt if hh2 == 0 else (), sem=gk)
                        E.dma("gpsimd", TT["z_bf"][hh2][0:16, 0:128],
                              z_ap[16 * hh2:16 * hh2 + 16, ci * 128:(ci + 1) * 128],
                              sem=gk)
                    rtok = (gk, E.cnt[gk])
                    for hh2 in range(2):
                        E.t("matmul", out=PS["ps_pan"][:, 0:128], lhsT=D["r16"][:],
                            rhs=TT["z_bf"][hh2][0:16, 0:128], start=True, stop=True,
                            waits=(rtok, ("v", E.cnt["v"]), ("c", 64)),
                            skip_group_check=True, incs=(("t", 1),))
                        E.v("tensor_scalar", out=TT["pan_z"][hh2][:], in0=PS["ps_pan"][:, 0:128],
                            scalar1=1.0, scalar2=None, op0=ALU.mult,
                            waits=(("t", E.cnt["t"]),), incs=(("v", 1),))
                        E.t("matmul", out=PS["ps_pan"][:, 0:128], lhsT=D["r16"][:],
                            rhs=TT["s_bf"][hh2][0:16, 0:128], start=True, stop=True,
                            waits=(("v", E.cnt["v"]),), skip_group_check=True,
                            incs=(("t", 1),))
                        E.v("tensor_scalar", out=TT["pan_s"][hh2][:], in0=PS["ps_pan"][:, 0:128],
                            scalar1=1.0, scalar2=None, op0=ALU.mult,
                            waits=(("t", E.cnt["t"]),), incs=(("v", 1),))
                    vwait(E)
                    pan_tok[hid] = E.cnt["v"]
                    if h == 0 and hid == 0 and DEBUG_L1A == "fpan":
                        d2 = D["dbg2"]
                        E.dma("gpsimd", d2[0:128, 0:128], TT["pan_z"][0][:],
                              waits=(("v", E.cnt["v"]),), sem="o0")
                        E.dma("gpsimd", d2[0:128, 128:256], TT["pan_s"][0][:], sem="o0")
                        E.dma("gpsimd", d2[0:16, 256:384], TT["z_bf"][0][0:16, :], sem="o0")
                        E.dma("gpsimd", d2[0:16, 384:512], TT["s_bf"][0][0:16, :], sem="o0")
                        E.v("memset", D["vdummy"][0:1, 0:8], 0.0,
                            waits=(("o0", E.cnt["o0"]),), incs=(("v", 1),))
                    cw = [wtok]
                    if hid >= 2:
                        cw.append(("t", mm_tok[hid - 2]))
                    E.s("activation", out=TT["w_deq"][par][:], in_=TT["w_int"][par][:],
                        func=AF.Copy, waits=tuple(cw), incs=(("s", 1),))
                    cvt = E.cnt["s"]
                    for hh2 in range(2):
                        E.v("tensor_tensor",
                            out=ap3(TT["w_deq"][par][:], hh2 * 2048, 128, 16, 1, 128),
                            in0=ap3(TT["w_deq"][par][:], hh2 * 2048, 128, 16, 1, 128),
                            in1=ap3(TT["pan_z"][hh2][:], 0, 0, 16, 1, 128),
                            op=ALU.subtract,
                            waits=(("s", cvt),) if hh2 == 0 else (), incs=(("v", 1),))
                        E.v("tensor_tensor",
                            out=ap3(TT["w_deq"][par][:], hh2 * 2048, 128, 16, 1, 128),
                            in0=ap3(TT["w_deq"][par][:], hh2 * 2048, 128, 16, 1, 128),
                            in1=ap3(TT["pan_s"][hh2][:], 0, 0, 16, 1, 128),
                            op=ALU.mult, incs=(("v", 1),))
                    deq_tok[hid] = E.cnt["v"]
                    ps = PS["psm0"] if mi == 0 else PS["psm1"]
                    for kc in range(32):
                        w_ = []
                        if kc == 0:
                            w_.append(("v", deq_tok[hid]))
                            if mi == 0:
                                war = sink_tok.get(ci - 1)
                                if war:
                                    w_.append(war)
                            if hid == 0:
                                w_.append(xtok)
                        E.t("matmul", out=ps[:],
                            lhsT=TT["w_deq"][par][:, kc * 128:(kc + 1) * 128],
                            rhs=xh[:, kc * TH:(kc + 1) * TH],
                            start=(kc == 0), stop=(kc == 31),
                            waits=tuple(w_), skip_group_check=True, incs=(("t", 1),))
                    mm_tok[hid] = E.cnt["t"]
                E.s("activation", out=TT["silu_t"][:], in_=PS["psm0"][:], func=AF.Silu,
                    waits=(("t", mm_tok[2 * ci + 1]),), incs=(("s", 1),))
                E.v("tensor_tensor", out=A0[:, ci * TH:(ci + 1) * TH], in0=PS["psm1"][:],
                    in1=TT["silu_t"][:], op=ALU.mult,
                    waits=(("s", E.cnt["s"]),), incs=(("v", 1),))
                sink_tok[ci] = ("v", E.cnt["v"])

        def phF2(E, h):
            T["guard"] = E.guard()
            if h == 0 and DEBUG_L1A == "fa0":
                dump_dbg2(E, D, A0)
            if h == 0 and DEBUG_L1A == "fa40":
                dump_dbg2(E, D, A0, off=40 * TH)
            emit_dn(E, nc, D, T, h, x_dst, final, PS,
                    dn_w=D["f_qd"].rearrange("(c p) m -> p c m", p=128),
                    dn_s=D["f_sd"], dn_z=D["f_zd"],
                    res_h=x_h, res_half=True,
                    s_tile=xh, sq=sq, r_pan=r_pan, mu_pan=mu_pan, fout=fout,
                    a_tile=A0, packed=False)

        for h in range(2):
            with contextlib.ExitStack() as pp:
                u2 = uid()
                sb2 = lambda n_, sh, dt: pp.enter_context(
                    nc.sbuf_tensor(f"{n_}_{u2}", list(sh), dt))
                TT.clear()
                TT["w_int"] = [sb2(f"wif{i}", (128, 4096), I32) for i in range(2)]
                TT["w_deq"] = [sb2(f"wdf{i}", (128, 4096), BF16) for i in range(2)]
                TT["s_bf"] = [sb2(f"sbff{j}", (16, 128), BF16) for j in range(2)]
                TT["z_bf"] = [sb2(f"zbff{j}", (16, 128), BF16) for j in range(2)]
                TT["pan_z"] = [sb2(f"pzf{i}", (128, 128), BF16) for i in range(2)]
                TT["pan_s"] = [sb2(f"pnsf{i}", (128, 128), BF16) for i in range(2)]
                TT["silu_t"] = sb2("silu_tf", (128, TH), BF16)
                PS.clear()
                PS["psm0"] = pp.enter_context(nc.psum_tensor(f"psm0_{uid()}", [128, TH], F32))
                PS["psm1"] = pp.enter_context(nc.psum_tensor(f"psm1_{uid()}", [128, TH], F32))
                PS["ps_pan"] = pp.enter_context(nc.psum_tensor(f"ps_pan_{uid()}", [128, TH], F32))
                E.run_phase(phF1, h)
            with contextlib.ExitStack() as pp:
                PS.clear()
                for i in range(8):
                    PS[f"ps8_{i}"] = pp.enter_context(
                        nc.psum_tensor(f"ps8y{i}_{uid()}", [128, TH], F32))
                T["dnT"] = alloc_dn_tiles(nc, pp, False)
                E.run_phase(phF2, h)


EMITTERS = {"l0a": emit_l0a, "l0f": emit_l0f, "l1a": emit_l1a, "l1f": emit_l1f}


SUBBLOCKS_PER_LAUNCH = 4
_launch_cache = {}


def get_launches(sub_blocks_all=None, per_launch=None):
    sub_blocks_all = list(SUBBLOCKS) if sub_blocks_all is None else sub_blocks_all
    per_launch = SUBBLOCKS_PER_LAUNCH if per_launch is None else per_launch
    key = (tuple(sub_blocks_all), per_launch)
    if key not in _launch_cache:
        groups = [sub_blocks_all[i:i + per_launch]
                  for i in range(0, len(sub_blocks_all), per_launch)]
        _launch_cache[key] = [(g, build_launch(g, gi == 0, gi == len(groups) - 1))
                              for gi, g in enumerate(groups)]
    return _launch_cache[key]


WKEYS = {
    "l0a": ["l0_w", "l0_m"],
    "l0f": ["l0f_at", "l0f_bt", "l0f_wrt"],
    "l1a": ["qkv_qw", "qkv_qz", "qkv_sc", "o_qw", "o_qz", "o_sc",
            "gu_qw", "gu_qz", "gu_sc", "dn_qw", "dn_qz", "dn_sc", "ln1_w", "ln2_w"],
    "l1f": ["f_qg", "f_qu", "f_qd", "f_sg", "f_zg", "f_su", "f_zu", "f_sd", "f_zd"],
}
CKEYS = ["c_r16", "c_ones1", "c_ones128", "c_ident"]


def run_all(S, cores, P, progs, n=8):
    x_cur = [c["x0"] for c in cores]
    first = True
    for (g, prog) in progs:
        in_maps = []
        for c in range(n):
            m = {k: S[k] for k in CKEYS}
            for sbn in g:
                for k in WKEYS[sbn]:
                    m[k] = S[k]
            m["x0" if first else "x_in"] = x_cur[c]
            in_maps.append(m)
        res = run_bass_kernel_spmd(prog, in_maps, core_ids=list(range(n)))
        x_cur = [np.asarray(res.results[c]["x_out"]) for c in range(n)]
        first = False
    return x_cur


def kernel(**inputs):
    S, cores, P, PI = prep_all(inputs)
    progs = get_launches()
    x_cur = run_all(S, cores, P, progs)
    out = np.empty((8192, 4096), np.float32)
    for c in range(8):
        out[c * TC:(c + 1) * TC][:, P] = x_cur[c].T.astype(np.float32)
    return out.reshape(2, 4096, 4096)
